# version 01 (final)
# baseline (speedup 1.0000x reference)
import numpy as np
import concourse.bass as bass
import concourse.mybir as mybir
from concourse.bass_utils import run_bass_kernel_spmd

F32 = mybir.dt.float32
BF16 = mybir.dt.bfloat16
AF = mybir.ActivationFunctionType
ALU = mybir.AluOpType
AX = mybir.AxisListType

D = 1024
NTOK = 2048
LNEPS = 1e-5
ENGS = ("sync", "gpsimd", "vector", "scalar", "tensor")
SEMS = ("LD", "LW", "CP", "SC", "MM", "GP", "ST")


class Prog:

    def __init__(self, nc):
        self.nc = nc
        self.ops = {e: [] for e in ENGS}
        self.counts = {s: 0 for s in SEMS}
        self.handles = {}

    def n(self, sem):
        return self.counts[sem]

    def emit(self, eng, fn, wait=(), inc=None, inc_k=1):
        wait = list(wait)
        if eng == 'vector':
            wait.append(('CP', self.counts['CP']))
            if inc is None:
                inc, inc_k = 'CP', 1
        for s, v in wait:
            assert v <= self.counts[s], f"wait {s}>={v} but only {self.counts[s]} recorded"
        self.ops[eng].append((tuple(wait), fn, inc, inc_k))
        if inc is not None:
            self.counts[inc] += inc_k
            return self.counts[inc]
        return None

    def dma(self, eng, out, in_, wait=(), inc='LD', **kw):
        w = list(wait) + [(inc, self.counts[inc])]
        return self.emit(eng, lambda q: q.dma_start(out=out, in_=in_, **kw),
                         wait=w, inc=inc, inc_k=16)

    def finalize(self, blk):
        for e in ENGS:
            if not self.ops[e]:
                continue
            ops = self.ops[e]
            def body(h, ops=ops):
                lastw = {}
                for wait, fn, inc, inc_k in ops:
                    for s, v in wait:
                        if lastw.get(s, -1) < v:
                            h.wait_ge(self.handles[s], v)
                            lastw[s] = v
                    inst = fn(h)
                    if inc is not None:
                        inst.then_inc(self.handles[inc], inc_k)
            getattr(blk, e)(body)

    def ln(self, x_ap, out_ap, nfeat, stat, c0, scr, eps_col, wait=(), out2=None):
        s_sum = stat[:, c0:c0 + 1]
        s_ssq = stat[:, c0 + 1:c0 + 2]
        s_msq = stat[:, c0 + 2:c0 + 3]
        self.emit('scalar', lambda s: s.activation(scr, x_ap, AF.Identity, accum_out=s_sum),
                  wait=wait)
        self.emit('scalar', lambda s: s.activation(scr, x_ap, AF.Square, accum_out=s_ssq))
        self.emit('scalar', lambda s: s.activation(s_msq, s_sum, AF.Square, scale=1.0 / nfeat),
                  inc='SC')
        scv = self.n('SC')
        self.emit('vector', lambda v: v.tensor_scalar(s_msq, s_ssq, 1.0 / nfeat, s_msq,
                                                      ALU.mult, ALU.subtract),
                  wait=[('SC', scv)], inc='CP')
        cpv = self.n('CP')
        self.emit('scalar', lambda s: s.activation(s_msq, s_msq, AF.Ln, bias=eps_col),
                  wait=[('CP', cpv)])
        self.emit('scalar', lambda s: s.activation(s_msq, s_msq, AF.Exp, scale=-0.5))
        self.emit('scalar', lambda s: s.activation(s_ssq, s_sum, AF.Identity, scale=1.0 / nfeat),
                  inc='SC')
        scv = self.n('SC')
        self.emit('vector', lambda v: v.tensor_scalar(out_ap, x_ap, s_ssq, s_msq,
                                                      ALU.subtract, ALU.mult),
                  wait=[('SC', scv)], inc='CP')
        if out2 is not None:
            self.emit('vector', lambda v: v.tensor_scalar(out2, x_ap, s_ssq, s_msq,
                                                          ALU.subtract, ALU.mult),
                      inc='CP')
        return self.n('CP')


def _mk_views(ap, nsl=4, ntile=4):
    return ap.rearrange("(s i p) f -> s p i f", s=nsl, i=ntile, p=128)


def build_phaseD(stage2=True):
    nc = bass.Bass("TRN2", target_bir_lowering=False, detect_race_conditions=False)
    x2r = nc.dram_tensor("x2r", [NTOK, D], F32, kind="ExternalInput").ap()
    atr = nc.dram_tensor("atr", [NTOK, D], F32, kind="ExternalInput").ap()
    pbrow = nc.dram_tensor("pbrow", [1, D], F32, kind="ExternalInput").ap()
    w1T = nc.dram_tensor("w1T", [D, 4096], F32, kind="ExternalInput").ap()
    b1cd = nc.dram_tensor("b1cd", [128, 32], F32, kind="ExternalInput").ap()
    w2T = nc.dram_tensor("w2T", [4096, D], F32, kind="ExternalInput").ap()
    b2cd = nc.dram_tensor("b2cd", [128, 8], F32, kind="ExternalInput").ap()
    idnd = nc.dram_tensor("idnd", [128, 128], F32, kind="ExternalInput").ap()
    outr = nc.dram_tensor("outr", [NTOK, D], F32, kind="ExternalOutput").ap()
    x3scr = nc.dram_tensor("x3scr", [NTOK, D], F32,
                           kind="Internal" if stage2 else "ExternalOutput").ap()

    TS = 512
    NSL = 4
    x2v, atv = _mk_views(x2r), _mk_views(atr)
    x3v, outv = _mk_views(x3scr), _mk_views(outr)
    w1v = w1T.rearrange("(k p) (g m) -> g p k m", k=8, p=128, g=8, m=512)
    w2v = w2T.rearrange("(k p) (m c) -> m p k c", k=32, p=128, m=8, c=128)

    from contextlib import ExitStack
    with ExitStack() as _st:
        def SB(*a):
            return _st.enter_context(nc.sbuf_tensor(*a))
        def PS(*a):
            return _st.enter_context(nc.psum_tensor(*a))
        def SEM(n):
            return _st.enter_context(nc.semaphore(n))
        xin = SB("xin", [128, 4, D], F32)
        cst = SB("cst", [128, 4], F32)
        pbrowt = SB("pbrowt", [1, D], F32)
        pbsb = SB("pbsb", [128, D], F32)
        ones1 = SB("ones1", [1, 128], F32)
        idnf = SB("idnf", [128, 128], F32)
        idn = SB("idn", [128, 128], BF16)
        x3t = SB("x3t", [128, 4, D], F32)
        x3b = SB("x3b", [128, 4, D], BF16)
        x3f = SB("x3f", [128, 8, NTOK], BF16)
        w1r = SB("w1r", [128, 2, 8, 512], BF16)
        b1c = SB("b1c", [128, 32], F32)
        b2c = SB("b2c", [128, 8], F32)
        hf = SB("hf", [128, 32, TS], BF16)
        w2r = SB("w2r", [128, 2, 32, 128], BF16)
        y2f = SB("y2f", [128, 8, TS], BF16)
        x3rr = SB("x3rr", [128, 4, D], F32)
        oot = SB("oot", [128, 4, D], F32)
        stat = SB("stat", [128, 8], F32)
        scr = SB("scr", [128, D], F32)
        ppb = PS("ppb", [128, D], F32)
        ptr = PS("ptr", [128, 2048], BF16)
        pmm = PS("pmm", [128, 1024], F32)
        ptr2 = PS("ptr2", [128, 2048], BF16)
        hLD = SEM("LD"); hLW = SEM("LW"); hCP = SEM("CP"); hSC = SEM("SC")
        hMM = SEM("MM"); hGP = SEM("GP"); hST = SEM("ST")
        blk = _st.enter_context(nc.Block())
        p = Prog(nc)
        p.handles = dict(LD=hLD, LW=hLW, CP=hCP, SC=hSC, MM=hMM, GP=hGP, ST=hST)

        p.dma('sync', pbrowt[:, :], pbrow[:, :])
        p.dma('sync', idnf[:, :], idnd[:, :])
        p.dma('sync', b1c[:, :], b1cd[:, :])
        p.dma('sync', b2c[:, :], b2cd[:, :])
        ld0 = p.n('LD')
        p.emit('gpsimd', lambda g: g.memset(ones1[:, :], 1.0))
        p.emit('gpsimd', lambda g: g.memset(cst[:, 0:1], LNEPS))
        p.emit('gpsimd', lambda g: g.tensor_copy(idn[:, :], idnf[:, :]),
               wait=[('LD', ld0)], inc='GP')
        gp0 = p.n('GP')
        for hh in range(2):
            p.emit('tensor', lambda t, hh=hh: t.matmul(
                ppb[:, hh * 512:(hh + 1) * 512], ones1[0:1, :],
                pbrowt[0:1, hh * 512:(hh + 1) * 512], start=True, stop=True),
                wait=[('LD', ld0), ('GP', gp0)], inc='MM')
        p.emit('vector', lambda v: v.tensor_copy(pbsb[:, :], ppb[:, :]),
               wait=[('MM', p.n('MM'))], inc='CP')
        cp_pb = p.n('CP')

        x3f_ready = {}
        prev_prep = None
        evac_ms = {}
        for sl in range(NSL):
            wait_in = list(prev_prep) if prev_prep else []
            p.dma('sync', xin[:, :, :], x2v[sl], wait=wait_in)
            p.dma('sync', x3t[:, :, :], atv[sl], wait=wait_in)
            ld_in = p.n('LD')
            for i in range(4):
                p.emit('vector', lambda v, i=i: v.tensor_tensor(
                    x3t[:, i, :], x3t[:, i, :], xin[:, i, :], ALU.add),
                    wait=[('LD', ld_in), ('CP', cp_pb)], inc='CP')
                p.emit('vector', lambda v, i=i: v.tensor_tensor(
                    x3t[:, i, :], x3t[:, i, :], pbsb[:, :], ALU.add), inc='CP')
                p.ln(x3t[:, i, :], x3t[:, i, :], D, stat, (i % 2) * 4, scr[:, :],
                     cst[:, 0:1], wait=[('CP', p.n('CP')), ('GP', gp0)])
                cp_ln = p.n('CP')
                p.emit('gpsimd', lambda g, i=i: g.tensor_copy(x3b[:, i, :], x3t[:, i, :]),
                       wait=[('CP', cp_ln)], inc='GP')
                gp_cast = p.n('GP')
                half = (i % 2) * 1024
                twait = [('GP', gp_cast)]
                if (i % 2) in evac_ms:
                    twait += [('CP', evac_ms[i % 2][0]), ('SC', evac_ms[i % 2][1])]
                col = sl * TS + i * 128
                for k in range(8):
                    p.emit('tensor', lambda t, i=i, k=k, half=half: t.transpose(
                        ptr[:, half + k * 128:half + (k + 1) * 128],
                        x3b[:, i, k * 128:(k + 1) * 128], idn[:, :]),
                        wait=(twait if k == 0 else ()),
                        inc=('MM' if k == 7 else None))
                mm_tr = p.n('MM')
                for k in range(8):
                    p.emit('vector', lambda v, k=k, half=half, col=col: v.tensor_copy(
                        x3f[:, k, col:col + 128],
                        ptr[:, half + k * 128:half + (k + 1) * 128]),
                        wait=([('MM', mm_tr)] if k == 0 else ()),
                        inc=('CP' if k == 7 else None))
                evac_ms[i % 2] = (p.n('CP'), p.n('SC'))
            p.dma('sync', x3v[sl], x3t[:, :, :],
                  wait=[('CP', p.n('CP')), ('SC', p.n('SC'))], inc='ST')
            prev_prep = [('CP', p.n('CP')), ('SC', p.n('SC')), ('GP', p.n('GP')),
                         ('MM', p.n('MM')), ('ST', 16 * (sl + 1))]
            x3f_ready[sl] = (p.n('CP'), p.n('SC'))

        if not stage2:
            p.emit('sync', lambda q: q.nop(), wait=[('ST', p.n('ST'))])
            p.finalize(blk)
            return nc
        pm_war = {}
        w1_war = {}
        w2_war = {}
        rr_war = None
        y2t_war = {}
        st_fin = 0
        for sl in range(NSL):
            cpv, scv = x3f_ready[sl]
            for g8 in range(8):
                idx = sl * 8 + g8
                slot = idx % 2
                lww = [w1_war[slot]] if slot in w1_war else []
                p.dma('gpsimd', w1r[:, slot, :, :], w1v[g8], wait=lww, inc='LW')
                lwv = p.n('LW')
                for m8 in range(4):
                    mt = g8 * 4 + m8
                    bank = (mt % 2) * 512
                    mmw = [('LW', lwv)]
                    if mt == 0:
                        mmw += [('CP', cpv), ('SC', scv)]
                    if (sl, 'w1', mt % 2) in pm_war:
                        mmw.append(pm_war[(sl, 'w1', mt % 2)])
                    elif sl > 0:
                        mmw.append(pm_war[(sl - 1, 'w2', mt % 2)])
                    for k in range(8):
                        p.emit('tensor', lambda t, m8=m8, k=k, bank=bank, sl=sl, slot=slot: t.matmul(
                            pmm[:, bank:bank + 512],
                            w1r[:, slot, k, m8 * 128:(m8 + 1) * 128],
                            x3f[:, k, sl * TS:(sl + 1) * TS],
                            start=(k == 0), stop=(k == 7)),
                            wait=(mmw if k == 0 else ()),
                            inc=('MM' if k == 7 else None))
                    mmv = p.n('MM')
                    p.emit('vector', lambda v, mt=mt, bank=bank: v.tensor_scalar(
                        hf[:, mt, :], pmm[:, bank:bank + 512], b1c[:, mt:mt + 1], 0.0,
                        ALU.add, ALU.max), wait=[('MM', mmv)], inc='CP')
                    pm_war[(sl, 'w1', mt % 2)] = ('CP', p.n('CP'))
                w1_war[slot] = ('MM', p.n('MM'))
            h_cp, h_sc = p.n('CP'), p.n('SC')
            for mt in range(8):
                idx = sl * 8 + mt
                slot = idx % 2
                lww = [w2_war[slot]] if slot in w2_war else []
                p.dma('gpsimd', w2r[:, slot, :, :], w2v[mt], wait=lww, inc='LW')
                lwv = p.n('LW')
                bank = (mt % 2) * 512
                mmw = [('LW', lwv), ('CP', h_cp), ('SC', h_sc)]
                mmw.append(pm_war[(sl, 'w2', mt % 2)] if (sl, 'w2', mt % 2) in pm_war
                           else pm_war[(sl, 'w1', mt % 2)])
                for k in range(32):
                    p.emit('tensor', lambda t, mt=mt, k=k, bank=bank, slot=slot: t.matmul(
                        pmm[:, bank:bank + 512],
                        w2r[:, slot, k, :], hf[:, k, :],
                        start=(k == 0), stop=(k == 31)),
                        wait=(mmw if k == 0 else ()),
                        inc=('MM' if k == 31 else None))
                w2_war[slot] = ('MM', p.n('MM'))
                mmv = p.n('MM')
                p.emit('vector', lambda v, mt=mt, bank=bank: v.tensor_scalar(
                    y2f[:, mt, :], pmm[:, bank:bank + 512], b2c[:, mt:mt + 1], None,
                    ALU.add), wait=[('MM', mmv)], inc='CP')
                pm_war[(sl, 'w2', mt % 2)] = ('CP', p.n('CP'))
            y2_cp, y2_sc = p.n('CP'), p.n('SC')
            rrw = [('ST', 16 * (sl + 1))]
            if rr_war:
                rrw += rr_war
            p.dma('sync', x3rr[:, :, :], x3v[sl], wait=rrw)
            ld_rr = p.n('LD')
            for i in range(4):
                half = (i % 2) * 1024
                twait = [('CP', y2_cp), ('SC', y2_sc)] if i == 0 else []
                if (i % 2) in y2t_war:
                    twait += [y2t_war[i % 2]]
                for k in range(8):
                    p.emit('tensor', lambda t, i=i, k=k, half=half: t.transpose(
                        ptr2[:, half + k * 128:half + (k + 1) * 128],
                        y2f[:, k, i * 128:(i + 1) * 128], idn[:, :]),
                        wait=(twait if k == 0 else ()),
                        inc=('MM' if k == 7 else None))
                mm_tr = p.n('MM')
                for k in range(8):
                    p.emit('vector', lambda v, i=i, k=k, half=half: v.tensor_tensor(
                        oot[:, i, k * 128:(k + 1) * 128],
                        ptr2[:, half + k * 128:half + (k + 1) * 128],
                        x3rr[:, i, k * 128:(k + 1) * 128], ALU.add),
                        wait=([('MM', mm_tr), ('LD', ld_rr)] if k == 0 else ()),
                        inc=('CP' if k == 7 else None))
                y2t_war[i % 2] = ('CP', p.n('CP'))
                p.ln(oot[:, i, :], oot[:, i, :], D, stat, 4 * (i % 2), scr[:, :],
                     cst[:, 0:1], wait=[('CP', p.n('CP')), ('GP', gp0)])
            p.dma('sync', outv[sl], oot[:, :, :],
                  wait=[('CP', p.n('CP')), ('SC', p.n('SC'))], inc='ST')
            st_fin = p.n('ST')
            rr_war = [('CP', p.n('CP'))]
        p.emit('sync', lambda q: q.nop(), wait=[('ST', st_fin)])
        p.finalize(blk)
    return nc


def _ident_bf16():
    return np.eye(128, dtype=np.float32)


def _colsplit(b, nm):
    return np.ascontiguousarray(b.reshape(nm, 128).T.astype(np.float32))


def run_phaseD(x2_full, attn_full, pD):
    nc = build_phaseD()
    idn = _ident_bf16()
    in_maps = []
    for c in range(8):
        sl = slice(c * NTOK, (c + 1) * NTOK)
        in_maps.append(dict(
            x2r=np.ascontiguousarray(x2_full[sl]),
            atr=np.ascontiguousarray(attn_full[sl]),
            pbrow=pD["proj_b"].reshape(1, D).astype(np.float32),
            w1T=np.ascontiguousarray(pD["w1"].astype(np.float32)),
            b1cd=_colsplit(pD["b1"], 32),
            w2T=np.ascontiguousarray(pD["w2"].astype(np.float32)),
            b2cd=_colsplit(pD["b2"], 8),
            idnd=idn,
        ))
    res = run_bass_kernel_spmd(nc, in_maps, core_ids=list(range(8)))
    return np.concatenate([r["outr"] for r in res.results], axis=0)


def build_phaseAB1():
    nc = bass.Bass("TRN2", target_bir_lowering=False, detect_race_conditions=False)
    x0r = nc.dram_tensor("x0r", [NTOK, D], F32, kind="ExternalInput").ap()
    pfr = nc.dram_tensor("pfr", [NTOK, D], F32, kind="ExternalInput").ap()
    pbr = nc.dram_tensor("pbr", [NTOK, D], F32, kind="ExternalInput").ap()
    borow = nc.dram_tensor("borow", [1, D], F32, kind="ExternalInput").ap()
    gTd = nc.dram_tensor("gTd", [D, 8], F32, kind="ExternalInput").ap()
    idnd = nc.dram_tensor("idnd", [128, 128], F32, kind="ExternalInput").ap()
    x1r = nc.dram_tensor("x1r", [NTOK, D], F32, kind="ExternalOutput").ap()
    logr = nc.dram_tensor("logr", [NTOK, 8], F32, kind="ExternalOutput").ap()
    cnt = nc.dram_tensor("cnt", [1, 8], F32, kind="ExternalOutput").ap()

    NSL = 4
    x0v, pfv, pbv = _mk_views(x0r), _mk_views(pfr), _mk_views(pbr)
    x1v = _mk_views(x1r)
    logv = logr.rearrange("(T p) e -> p T e", T=16, p=128)

    from contextlib import ExitStack
    with ExitStack() as _st:
        SB = lambda *a: _st.enter_context(nc.sbuf_tensor(*a))
        PS = lambda *a: _st.enter_context(nc.psum_tensor(*a))
        SEM = lambda n: _st.enter_context(nc.semaphore(n))
        xin = SB("xin", [128, 4, D], F32)
        acc = SB("acc", [128, 4, D], F32)
        borowt = SB("borowt", [1, D], F32)
        bosb = SB("bosb", [128, D], F32)
        ones1 = SB("ones1", [1, 128], F32)
        onescol = SB("onescol", [128, 1], F32)
        idnf = SB("idnf", [128, 128], F32)
        idn = SB("idn", [128, 128], BF16)
        cst = SB("cst", [128, 4], F32)
        stat = SB("stat", [128, 8], F32)
        scr = SB("scr", [128, D], F32)
        x1b = SB("x1b", [128, 4, D], BF16)
        x1fs = SB("x1fs", [128, 8, 512], BF16)
        g8 = SB("g8", [128, 8, 8], F32)
        g8b = SB("g8b", [128, 8, 8], BF16)
        logsb = SB("logsb", [128, 16, 8], F32)
        rkacc = SB("rkacc", [128, 16, 8], F32)
        pick = SB("pick", [128, 16, 8], F32)
        cnt8 = SB("cnt8", [1, 8], F32)
        ppb = PS("ppb", [128, D], F32)
        ptr = PS("ptr", [128, 2048], BF16)
        pg = PS("pg", [128, 16, 8], F32)
        pcnt = PS("pcnt", [1, 128], F32)
        hands = {}
        for s_ in SEMS:
            hands[s_] = SEM(s_)
        blk = _st.enter_context(nc.Block())
        p = Prog(nc)
        p.handles = hands

        p.dma('sync', borowt[:, :], borow[:, :])
        p.dma('sync', idnf[:, :], idnd[:, :])
        p.dma('sync', g8[:, :, :], gTd.rearrange("(k p) e -> p k e", k=8, p=128))
        ld0 = p.n('LD')
        p.emit('gpsimd', lambda g: g.memset(ones1[:, :], 1.0))
        p.emit('gpsimd', lambda g: g.memset(onescol[:, :], 1.0))
        p.emit('gpsimd', lambda g: g.memset(cst[:, 0:1], LNEPS))
        p.emit('gpsimd', lambda g: g.tensor_copy(idn[:, :], idnf[:, :]),
               wait=[('LD', ld0)])
        p.emit('gpsimd', lambda g: g.tensor_copy(g8b[:, :, :], g8[:, :, :]), inc='GP')
        gp0 = p.n('GP')
        for hh in range(2):
            p.emit('tensor', lambda t, hh=hh: t.matmul(
                ppb[:, hh * 512:(hh + 1) * 512], ones1[0:1, :],
                borowt[0:1, hh * 512:(hh + 1) * 512], start=True, stop=True),
                wait=[('LD', ld0), ('GP', gp0)], inc='MM')
        p.emit('vector', lambda v: v.tensor_copy(bosb[:, :], ppb[:, :]),
               wait=[('MM', p.n('MM'))], inc='CP')
        cp_pb = p.n('CP')

        prev = None
        evac_ms = {}
        for sl in range(NSL):
            wl = list(prev) if prev else []
            p.dma('sync', xin[:, :, :], x0v[sl], wait=wl)
            p.dma('sync', acc[:, :, :], pfv[sl], wait=wl)
            ld_in = p.n('LD')
            for i in range(4):
                p.emit('vector', lambda v, i=i: v.tensor_tensor(
                    acc[:, i, :], acc[:, i, :], xin[:, i, :], ALU.add),
                    wait=[('LD', ld_in), ('CP', cp_pb)], inc='CP')
            cp_a1 = p.n('CP')
            p.dma('sync', xin[:, :, :], pbv[sl], wait=[('CP', cp_a1)])
            ld_in2 = p.n('LD')
            for i in range(4):
                p.emit('vector', lambda v, i=i: v.tensor_tensor(
                    acc[:, i, :], acc[:, i, :], xin[:, i, :], ALU.add),
                    wait=[('LD', ld_in2)], inc='CP')
                p.emit('vector', lambda v, i=i: v.tensor_tensor(
                    acc[:, i, :], acc[:, i, :], bosb[:, :], ALU.add), inc='CP')
                p.ln(acc[:, i, :], acc[:, i, :], D, stat, (i % 2) * 4, scr[:, :],
                     cst[:, 0:1], wait=[('CP', p.n('CP')), ('GP', gp0)])
                cp_ln = p.n('CP')
                p.emit('gpsimd', lambda g, i=i: g.tensor_copy(x1b[:, i, :], acc[:, i, :]),
                       wait=[('CP', cp_ln)], inc='GP')
                gp_cast = p.n('GP')
                half = (i % 2) * 1024
                twait = [('GP', gp_cast)]
                if (i % 2) in evac_ms:
                    twait += [('CP', evac_ms[i % 2])]
                for k in range(8):
                    p.emit('tensor', lambda t, i=i, k=k, half=half: t.transpose(
                        ptr[:, half + k * 128:half + (k + 1) * 128],
                        x1b[:, i, k * 128:(k + 1) * 128], idn[:, :]),
                        wait=(twait if k == 0 else ()),
                        inc=('MM' if k == 7 else None))
                mm_tr = p.n('MM')
                for k in range(8):
                    p.emit('vector', lambda v, k=k, half=half, i=i: v.tensor_copy(
                        x1fs[:, k, i * 128:(i + 1) * 128],
                        ptr[:, half + k * 128:half + (k + 1) * 128]),
                        wait=([('MM', mm_tr)] if k == 0 else ()),
                        inc=('CP' if k == 7 else None))
                evac_ms[i % 2] = p.n('CP')
            cp_tr = p.n('CP')
            p.dma('sync', x1v[sl], acc[:, :, :],
                  wait=[('CP', cp_tr), ('SC', p.n('SC'))], inc='ST')
            for i in range(4):
                for k in range(8):
                    p.emit('tensor', lambda t, i=i, k=k, sl=sl: t.matmul(
                        pg[:, sl * 4 + i, :], x1fs[:, k, i * 128:(i + 1) * 128],
                        g8b[:, k, :], start=(k == 0), stop=(k == 7)),
                        wait=([('CP', cp_tr)] if (i == 0 and k == 0) else ()),
                        inc=('MM' if k == 7 else None))
            mm_g = p.n('MM')
            for i in range(4):
                p.emit('vector', lambda v, i=i, sl=sl: v.tensor_copy(
                    logsb[:, sl * 4 + i, :], pg[:, sl * 4 + i, :]),
                    wait=([('MM', mm_g)] if i == 0 else ()),
                    inc=('CP' if i == 3 else None))
            prev = [('CP', p.n('CP')), ('SC', p.n('SC')), ('GP', p.n('GP')),
                    ('MM', p.n('MM')), ('ST', 16 * (sl + 1))]
        cp_log = p.n('CP')
        p.dma('sync', logv, logsb[:, :, :], wait=[('CP', cp_log)], inc='ST')
        p.emit('gpsimd', lambda g: g.memset(rkacc[:, :, :], 0.0), inc='GP')
        gp_rk = p.n('GP')
        cpw = [('CP', cp_log), ('GP', gp_rk)]
        for e in range(8):
            p.emit('vector', lambda v, e=e: v.tensor_tensor(
                pick[:, :, :], logsb[:, :, e:e + 1].broadcast_to([128, 16, 8]),
                logsb[:, :, :], ALU.is_gt), wait=(cpw if e == 0 else ()))
            p.emit('vector', lambda v: v.tensor_tensor(
                rkacc[:, :, :], rkacc[:, :, :], pick[:, :, :], ALU.add))
            if e < 7:
                p.emit('vector', lambda v, e=e: v.tensor_tensor(
                    pick[:, :, e + 1:8], logsb[:, :, e:e + 1].broadcast_to([128, 16, 7 - e]),
                    logsb[:, :, e + 1:8], ALU.is_equal))
                p.emit('vector', lambda v, e=e: v.tensor_tensor(
                    rkacc[:, :, e + 1:8], rkacc[:, :, e + 1:8], pick[:, :, e + 1:8], ALU.add))
        p.emit('vector', lambda v: v.tensor_scalar(
            pick[:, :, :], rkacc[:, :, :], 2.0, None, ALU.is_lt), inc='CP')
        cp_pick = p.n('CP')
        p.emit('tensor', lambda t: t.matmul(
            pcnt[:, :], onescol[:, :],
            pick[:, :, :].rearrange("p T e -> p (T e)"), start=True, stop=True),
            wait=[('CP', cp_pick)], inc='MM')
        p.emit('vector', lambda v: v.tensor_reduce(
            cnt8[0:1, 0:8], pcnt[0:1, :].rearrange("o (T e) -> o e T", T=16, e=8),
            AX.X, ALU.add), wait=[('MM', p.n('MM'))], inc='CP')
        p.dma('sync', cnt[:, :], cnt8[:, :], wait=[('CP', p.n('CP'))], inc='ST')
        p.emit('sync', lambda q: q.nop(), wait=[('ST', p.n('ST'))])
        p.finalize(blk)
    return nc


def build_phaseB2(n_exp=8, do_shared=True, dbg=False):
    nc = bass.Bass("TRN2", target_bir_lowering=False, detect_race_conditions=False)
    x1r = nc.dram_tensor("x1r", [NTOK, D], F32, kind="ExternalInput").ap()
    logrd = nc.dram_tensor("logrd", [NTOK, 8], F32, kind="ExternalInput").ap()
    cntalld = nc.dram_tensor("cntalld", [8, 8], F32, kind="ExternalInput").ap()
    maskcold = nc.dram_tensor("maskcold", [8, 1], F32, kind="ExternalInput").ap()
    wegT = nc.dram_tensor("wegT", [8, D, D], F32, kind="ExternalInput").ap()
    weuT = nc.dram_tensor("weuT", [8, D, D], F32, kind="ExternalInput").ap()
    wedT = nc.dram_tensor("wedT", [8, D, D], F32, kind="ExternalInput").ap()
    wsgT = nc.dram_tensor("wsgT", [D, 2048], F32, kind="ExternalInput").ap()
    wsuT = nc.dram_tensor("wsuT", [D, 2048], F32, kind="ExternalInput").ap()
    wsdT = nc.dram_tensor("wsdT", [2048, D], F32, kind="ExternalInput").ap()
    idnd = nc.dram_tensor("idnd", [128, 128], F32, kind="ExternalInput").ap()
    strUd = nc.dram_tensor("strUd", [128, 128], F32, kind="ExternalInput").ap()
    x2r = nc.dram_tensor("x2r", [NTOK, D], F32, kind="ExternalOutput").ap()
    combscr = nc.dram_tensor("combscr", [128, 128], F32,
                             kind="ExternalOutput" if dbg else "Internal").ap()
    yacd = (nc.dram_tensor("yacd", [128, 8, 1024], F32, kind="ExternalOutput").ap()
            if dbg else None)

    NSL = 4
    x1v = _mk_views(x1r)
    x2v = _mk_views(x2r)
    logv = logrd.rearrange("(T p) e -> p T e", T=16, p=128)
    wegv = wegT.rearrange("e (k p) m -> e p k m", k=8, p=128)
    weuv = weuT.rearrange("e (k p) m -> e p k m", k=8, p=128)
    wedv = wedT.rearrange("e (k p) m -> e p k m", k=8, p=128)
    wsgv = wsgT.rearrange("(k p) (h m) -> h p k m", k=8, p=128, h=2, m=1024)
    wsuv = wsuT.rearrange("(k p) (h m) -> h p k m", k=8, p=128, h=2, m=1024)
    wsdv = wsdT.rearrange("(h k p) m -> h p k m", h=2, k=8, p=128)

    from contextlib import ExitStack
    with ExitStack() as _st:
        SB = lambda *a: _st.enter_context(nc.sbuf_tensor(*a))
        PS = lambda *a: _st.enter_context(nc.psum_tensor(*a))
        SEM = lambda n: _st.enter_context(nc.semaphore(n))
        x1f = SB("x1f", [128, 8, NTOK], BF16)
        yac = SB("yac", [128, 8, 1024], F32)
        asb = SB("asb", [128, 8, 1024], BF16)
        gsb = SB("gsb", [128, 8, 1024], BF16)
        wring = SB("wring", [128, 2, 8, 1024], BF16)
        xin = SB("xin", [128, 4, D], F32)
        acc = SB("acc", [128, 4, D], F32)
        x1b = SB("x1b", [128, 4, D], BF16)
        cbsb = SB("cbsb", [128, 1024], F32)
        logsb = SB("logsb", [128, 16, 8], F32)
        sc = SB("sc", [128, 16, 8], F32)
        rkacc = SB("rkacc", [128, 16, 8], F32)
        pick = SB("pick", [128, 16, 8], F32)
        cum = SB("cum", [128, 16, 8], F32)
        rstE = SB("rstE", [1, 128], F32)
        strU = SB("strU", [128, 128], F32)
        totrow = SB("totrow", [1, 128], F32)
        totE = SB("totE", [1, 128], F32)
        exclE = SB("exclE", [1, 128], F32)
        exclrow = SB("exclrow", [1, 128], F32)
        pbfull = SB("pbfull", [128, 128], F32)
        r16a = SB("r16a", [128, 16], F32)
        r16b = SB("r16b", [128, 16], F32)
        cntall = SB("cntall", [8, 8], F32)
        maskcol = SB("maskcol", [8, 1], F32)
        cpre = SB("cpre", [1, 8], F32)
        pb8 = SB("pb8", [128, 8], F32)
        combb = SB("combb", [128, 128], F32)
        combT = SB("combT", [128, 128], F32)
        cfrow = SB("cfrow", [1, NTOK], F32)
        ones1 = SB("ones1", [1, 128], F32)
        ones1b = SB("ones1b", [1, 128], BF16)
        onescol = SB("onescol", [128, 1], F32)
        idnf = SB("idnf", [128, 128], F32)
        idn = SB("idn", [128, 128], BF16)
        cst = SB("cst", [128, 4], F32)
        stat = SB("stat", [128, 8], F32)
        scr = SB("scr", [128, D], F32)
        pmm = PS("pmm", [128, 1024], F32)
        ptr = PS("ptr", [128, 2048], BF16)
        ptf = PS("ptf", [128, 128], F32)
        pcb = PS("pcb", [128, 512], F32)
        pr8 = PS("pr8", [128, 128], F32)
        pc8 = PS("pc8", [1, 8], F32)
        hands = {}
        for s_ in SEMS:
            hands[s_] = SEM(s_)
        blk = _st.enter_context(nc.Block())
        p = Prog(nc)
        p.handles = hands

        p.dma('sync', idnf[:, :], idnd[:, :])
        p.dma('sync', strU[:, :], strUd[:, :])
        p.dma('sync', cntall[:, :], cntalld[:, :])
        p.dma('sync', maskcol[:, :], maskcold[:, :])
        ld0 = p.n('LD')
        p.emit('gpsimd', lambda g: g.memset(ones1[:, :], 1.0))
        p.emit('gpsimd', lambda g: g.memset(ones1b[:, :], 1.0))
        p.emit('gpsimd', lambda g: g.memset(onescol[:, :], 1.0))
        p.emit('gpsimd', lambda g: g.memset(cst[:, 0:1], LNEPS))
        p.emit('gpsimd', lambda g: g.memset(cst[:, 1:2], 1e-20))
        p.emit('gpsimd', lambda g: g.memset(rstE[:, :], 1.0))
        p.emit('gpsimd', lambda g: g.memset(
            rstE[:, :].rearrange("o (e T) -> o e T", e=8, T=16)[:, :, 0:1], 0.0))
        p.emit('gpsimd', lambda g: g.tensor_copy(idn[:, :], idnf[:, :]),
               wait=[('LD', ld0)], inc='GP')
        gp0 = p.n('GP')

        evac_ms = {}
        prev = None
        for sl in range(NSL):
            wl = list(prev) if prev else []
            p.dma('sync', xin[:, :, :], x1v[sl], wait=wl)
            ld_in = p.n('LD')
            for i in range(4):
                p.emit('gpsimd', lambda g, i=i: g.tensor_copy(x1b[:, i, :], xin[:, i, :]),
                       wait=[('LD', ld_in), ('GP', gp0)], inc='GP')
                gp_cast = p.n('GP')
                half = (i % 2) * 1024
                twait = [('GP', gp_cast)]
                if (i % 2) in evac_ms:
                    twait += [('CP', evac_ms[i % 2])]
                col = sl * 512 + i * 128
                for k in range(8):
                    p.emit('tensor', lambda t, i=i, k=k, half=half: t.transpose(
                        ptr[:, half + k * 128:half + (k + 1) * 128],
                        x1b[:, i, k * 128:(k + 1) * 128], idn[:, :]),
                        wait=(twait if k == 0 else ()),
                        inc=('MM' if k == 7 else None))
                mm_tr = p.n('MM')
                for k in range(8):
                    p.emit('vector', lambda v, k=k, half=half, col=col: v.tensor_copy(
                        x1f[:, k, col:col + 128],
                        ptr[:, half + k * 128:half + (k + 1) * 128]),
                        wait=([('MM', mm_tr)] if k == 0 else ()),
                        inc=('CP' if k == 7 else None))
                evac_ms[i % 2] = p.n('CP')
            prev = [('CP', p.n('CP')), ('GP', p.n('GP'))]
        x1f_cp = p.n('CP')

        p.dma('sync', logsb[:, :, :], logv)
        ld_log = p.n('LD')
        lg = logsb[:, :, :]
        p.emit('vector', lambda v: v.tensor_reduce(r16a[:, :].unsqueeze(2), lg, AX.X, ALU.max),
               wait=[('LD', ld_log)])
        p.emit('vector', lambda v: v.tensor_tensor(
            sc[:, :, :], lg, r16a[:, :].unsqueeze(2).broadcast_to([128, 16, 8]),
            ALU.subtract), inc='CP')
        p.emit('scalar', lambda s: s.activation(sc[:, :, :], sc[:, :, :], AF.Exp),
               wait=[('CP', p.n('CP'))], inc='SC')
        p.emit('vector', lambda v: v.tensor_reduce(r16a[:, :].unsqueeze(2), sc[:, :, :],
                                                   AX.X, ALU.add),
               wait=[('SC', p.n('SC'))])
        p.emit('vector', lambda v: v.reciprocal(r16b[:, :], r16a[:, :]))
        p.emit('vector', lambda v: v.tensor_tensor(
            sc[:, :, :], sc[:, :, :], r16b[:, :].unsqueeze(2).broadcast_to([128, 16, 8]),
            ALU.mult), inc='CP')
        cp_sc = p.n('CP')
        p.emit('gpsimd', lambda g: g.memset(rkacc[:, :, :], 0.0), inc='GP')
        gp_rk = p.n('GP')
        w0 = [('CP', cp_sc), ('GP', gp_rk)]
        for e in range(8):
            p.emit('vector', lambda v, e=e: v.tensor_tensor(
                pick[:, :, :], sc[:, :, e:e + 1].broadcast_to([128, 16, 8]),
                sc[:, :, :], ALU.is_gt), wait=(w0 if e == 0 else ()))
            p.emit('vector', lambda v: v.tensor_tensor(
                rkacc[:, :, :], rkacc[:, :, :], pick[:, :, :], ALU.add))
            if e < 7:
                p.emit('vector', lambda v, e=e: v.tensor_tensor(
                    pick[:, :, e + 1:8], sc[:, :, e:e + 1].broadcast_to([128, 16, 7 - e]),
                    sc[:, :, e + 1:8], ALU.is_equal))
                p.emit('vector', lambda v, e=e: v.tensor_tensor(
                    rkacc[:, :, e + 1:8], rkacc[:, :, e + 1:8], pick[:, :, e + 1:8], ALU.add))
        p.emit('vector', lambda v: v.tensor_scalar(
            pick[:, :, :], rkacc[:, :, :], 2.0, None, ALU.is_lt), inc='CP')
        p.emit('vector', lambda v: v.tensor_tensor(
            rkacc[:, :, :], sc[:, :, :], pick[:, :, :], ALU.mult))
        p.emit('vector', lambda v: v.tensor_reduce(r16a[:, :].unsqueeze(2),
                                                   rkacc[:, :, :], AX.X, ALU.add))
        p.emit('vector', lambda v: v.tensor_scalar(r16a[:, :], r16a[:, :],
                                                   cst[:, 1:2], None, ALU.add))
        p.emit('vector', lambda v: v.reciprocal(r16b[:, :], r16a[:, :]))
        cp_pk = p.n('CP')
        p.emit('tensor', lambda t: t.matmul(
            ptf[0:1, :], onescol[:, :], pick[:, :, :].rearrange("p T e -> p (T e)"),
            start=True, stop=True), wait=[('CP', cp_pk)])
        p.emit('tensor', lambda t: t.matmul(pc8[:, :], maskcol[:, :], cntall[:, :],
                                            start=True, stop=True), inc='MM')
        mm_tot = p.n('MM')
        p.emit('vector', lambda v: v.tensor_copy(totrow[:, :], ptf[0:1, :]),
               wait=[('MM', mm_tot)])
        p.emit('vector', lambda v: v.tensor_copy(cpre[:, :], pc8[:, :]))
        p.emit('vector', lambda v: v.tensor_copy(
            totE[:, :].rearrange("o (e T) -> o e T", e=8, T=16),
            totrow[:, :].rearrange("o (T e) -> o T e", T=16, e=8).transpose([0, 2, 1])))
        p.emit('vector', lambda v: v.tensor_tensor_scan(
            exclE[:, :], rstE[:, :], totE[:, :], 0.0, ALU.mult, ALU.add))
        p.emit('vector', lambda v: v.tensor_tensor(
            exclE[:, :], exclE[:, :], totE[:, :], ALU.subtract))
        p.emit('vector', lambda v: v.tensor_copy(
            exclrow[:, :].rearrange("o (T e) -> o T e", T=16, e=8),
            exclE[:, :].rearrange("o (e T) -> o e T", e=8, T=16).transpose([0, 2, 1])))
        p.emit('vector', lambda v: v.tensor_tensor(
            exclrow[:, :].rearrange("o (T e) -> o T e", T=16, e=8),
            exclrow[:, :].rearrange("o (T e) -> o T e", T=16, e=8),
            cpre[:, :].unsqueeze(1).broadcast_to([1, 16, 8]), ALU.add), inc='CP')
        cp_excl = p.n('CP')
        p.emit('tensor', lambda t: t.matmul(pr8[:, :], ones1[0:1, :], exclrow[0:1, :],
                                            start=True, stop=False),
               wait=[('CP', cp_excl)])
        for T in range(16):
            p.emit('tensor', lambda t, T=T: t.matmul(
                pr8[:, T * 8:(T + 1) * 8], strU[:, :], pick[:, T, :],
                start=False, stop=(T == 15)),
                inc=('MM' if T == 15 else None))
        p.emit('vector', lambda v: v.tensor_copy(pbfull[:, :], pr8[:, :]),
               wait=[('MM', p.n('MM'))])
        p.emit('vector', lambda v: v.tensor_scalar(
            pbfull[:, :], pbfull[:, :], 5120.0, None, ALU.is_lt))
        p.emit('vector', lambda v: v.tensor_tensor(
            pick[:, :, :], pick[:, :, :],
            pbfull[:, :].rearrange("p (T e) -> p T e", T=16, e=8), ALU.mult))
        p.emit('vector', lambda v: v.tensor_tensor(
            pick[:, :, :], pick[:, :, :], rkacc[:, :, :], ALU.mult))
        p.emit('vector', lambda v: v.tensor_tensor(
            combb[:, :].rearrange("p (T e) -> p T e", T=16, e=8),
            pick[:, :, :],
            r16b[:, :].unsqueeze(2).broadcast_to([128, 16, 8]),
            ALU.mult), inc='CP')
        cp_comb = p.n('CP')
        p.emit('tensor', lambda t: t.transpose(ptf[:, :], combb[:, :], idnf[:, :]),
               wait=[('CP', cp_comb)], inc='MM')
        p.emit('vector', lambda v: v.tensor_copy(combT[:, :], ptf[:, :]),
               wait=[('MM', p.n('MM'))], inc='CP')
        p.dma('sync', combscr[:, :], combT[:, :],
              wait=[('CP', p.n('CP'))], inc='ST')
        st_comb = p.n('ST')

        ring_war = {}
        ring_seq = [0]
        pm_war = {}
        gsb_war = None
        asb_war = None
        yac_war = None
        cb_war = None
        st_x2 = 0
        x2_war = None

        def load_w(view_ap, wait_extra=()):
            slot = ring_seq[0] % 2
            wl = list(wait_extra)
            if slot in ring_war:
                wl.append(ring_war[slot])
            p.dma('gpsimd', wring[:, slot, :, :], view_ap, wait=wl, inc='LW')
            ring_seq[0] += 1
            return slot, p.n('LW')

        def mm_block(slot, lwv, rhs_tiles, extra_wait, nK, tag, evac_fn):
            for mt in range(8):
                bank = (mt % 2) * 512
                w = [('LW', lwv)] + (list(extra_wait) if mt == 0 else [])
                if (tag, mt % 2) in pm_war:
                    w.append(pm_war[(tag, mt % 2)])
                for k in range(nK):
                    p.emit('tensor', lambda t, mt=mt, k=k, bank=bank, slot=slot,
                           rhs_tiles=rhs_tiles: t.matmul(
                        pmm[:, bank:bank + 512], wring[:, slot, k % 8, mt * 128:(mt + 1) * 128],
                        rhs_tiles(k), start=(k == 0), stop=(k == nK - 1)),
                        wait=(w if k == 0 else ()),
                        inc=('MM' if k == nK - 1 else None))
                evac_fn(mt, bank, p.n('MM'))
                pm_war[(tag, mt % 2)] = ('CP', p.n('CP'))

        for half in range(2):
            hbase = half * 1024
            p.emit('gpsimd', lambda g: g.memset(yac[:, :, :], 0.0),
                   wait=([yac_war] if yac_war else []), inc='GP')
            gp_y = p.n('GP')
            yac_cp_last = None
            for e in range(n_exp):
                cbw = [('ST', st_comb), ('GP', gp0)]
                if cb_war:
                    cbw.append(cb_war)
                p.dma('scalar', cfrow[:, :].rearrange("o (T p) -> o T p", T=16, p=128),
                      combscr.rearrange("(T e) p -> e T p", T=16, e=8)[e],
                      wait=cbw, inc='LD')
                ld_cf = p.n('LD')
                for t2 in range(2):
                    mw = [('LD', ld_cf)] if t2 == 0 else [('CP', p.n('CP'))]
                    p.emit('tensor', lambda t, e=e, t2=t2, hbase=hbase: t.matmul(
                        pcb[:, :], ones1[0:1, :],
                        cfrow[0:1, hbase + t2 * 512:hbase + (t2 + 1) * 512],
                        start=True, stop=True),
                        wait=mw, inc='MM')
                    mmv = p.n('MM')
                    p.emit('vector', lambda v, t2=t2: v.tensor_copy(
                        cbsb[:, t2 * 512:(t2 + 1) * 512], pcb[:, :]),
                        wait=[('MM', mmv)], inc='CP')
                cb_cp = p.n('CP')
                slot, lwv = load_w(wegv[e])
                gw = [('CP', x1f_cp)]
                if gsb_war:
                    gw.append(gsb_war)
                for ts2 in range(2):
                    cols = hbase + ts2 * 512
                    def ev_g(mt, bank, mmv, ts2=ts2):
                        p.emit('vector', lambda v, mt=mt, bank=bank, ts2=ts2: v.tensor_copy(
                            gsb[:, mt, ts2 * 512:(ts2 + 1) * 512], pmm[:, bank:bank + 512]),
                            wait=[('MM', mmv)], inc='CP')
                    mm_block(slot, lwv, lambda k, cols=cols: x1f[:, k, cols:cols + 512],
                             gw if ts2 == 0 else [], 8, 'g', ev_g)
                ring_war[slot] = ('MM', p.n('MM'))
                g_cp = p.n('CP')
                silu_w = [('CP', g_cp)]
                if asb_war:
                    silu_w.append(asb_war)
                p.emit('scalar', lambda s: s.activation(
                    asb[:, :, :], gsb[:, :, :], AF.Silu), wait=silu_w, inc='SC')
                silu_sc = p.n('SC')
                slot, lwv = load_w(weuv[e])
                for ts2 in range(2):
                    cols = hbase + ts2 * 512
                    def ev_u(mt, bank, mmv, ts2=ts2):
                        p.emit('vector', lambda v, mt=mt, bank=bank, ts2=ts2:
                               v.scalar_tensor_tensor(
                            asb[:, mt, ts2 * 512:(ts2 + 1) * 512], pmm[:, bank:bank + 512],
                            0.0, asb[:, mt, ts2 * 512:(ts2 + 1) * 512],
                            ALU.bypass, ALU.mult),
                            wait=[('MM', mmv)], inc='CP')
                    mm_block(slot, lwv, lambda k, cols=cols: x1f[:, k, cols:cols + 512],
                             [('SC', silu_sc)] if ts2 == 0 else [], 8, 'g', ev_u)
                ring_war[slot] = ('MM', p.n('MM'))
                u_cp = p.n('CP')
                p.emit('gpsimd', lambda g: g.tensor_tensor(
                    asb[:, :, :], asb[:, :, :],
                    cbsb[:, :].unsqueeze(1).broadcast_to([128, 8, 1024]), ALU.mult),
                    wait=[('CP', u_cp), ('GP', gp_y)], inc='GP')
                cbm_gp = p.n('GP')
                slot, lwv = load_w(wedv[e])
                for ts2 in range(2):
                    def ev_d(mt, bank, mmv, ts2=ts2):
                        p.emit('vector', lambda v, mt=mt, bank=bank, ts2=ts2: v.tensor_tensor(
                            yac[:, mt, ts2 * 512:(ts2 + 1) * 512], pmm[:, bank:bank + 512],
                            yac[:, mt, ts2 * 512:(ts2 + 1) * 512], ALU.add),
                            wait=[('MM', mmv)], inc='CP')
                    mm_block(slot, lwv,
                             lambda k, ts2=ts2: asb[:, k, ts2 * 512:(ts2 + 1) * 512],
                             [('GP', cbm_gp)] if ts2 == 0 else [], 8, 'g', ev_d)
                ring_war[slot] = ('MM', p.n('MM'))
                dn_cp = p.n('CP')
                gsb_war = ('SC', silu_sc)
                asb_war = ('MM', p.n('MM'))
                cb_war = ('GP', cbm_gp)
                yac_cp_last = ('CP', dn_cp)
            hs = [gsb, asb]
            if not do_shared:
                y_cp = p.n('CP')
            sh_skip = not do_shared
            for m2 in (range(2) if do_shared else []):
                slot, lwv = load_w(wsgv[m2])
                for ts2 in range(2):
                    cols = hbase + ts2 * 512
                    def ev_hg(mt, bank, mmv, ts2=ts2, m2=m2):
                        p.emit('vector', lambda v, mt=mt, bank=bank, ts2=ts2, m2=m2:
                               v.tensor_copy(
                            hs[m2][:, mt, ts2 * 512:(ts2 + 1) * 512], pmm[:, bank:bank + 512]),
                            wait=[('MM', mmv)], inc='CP')
                    mm_block(slot, lwv, lambda k, cols=cols: x1f[:, k, cols:cols + 512],
                             [asb_war, ('CP', p.n('CP')), ('SC', p.n('SC')),
                              ('GP', p.n('GP'))] if ts2 == 0 and m2 == 0 else [],
                             8, 'g', ev_hg)
                ring_war[slot] = ('MM', p.n('MM'))
                hg_cp = p.n('CP')
                p.emit('scalar', lambda s, m2=m2: s.activation(
                    hs[m2][:, :, :], hs[m2][:, :, :], AF.Silu),
                    wait=[('CP', hg_cp)], inc='SC')
                silu_sc = p.n('SC')
                slot, lwv = load_w(wsuv[m2])
                for ts2 in range(2):
                    cols = hbase + ts2 * 512
                    def ev_hu(mt, bank, mmv, ts2=ts2, m2=m2):
                        p.emit('vector', lambda v, mt=mt, bank=bank, ts2=ts2, m2=m2:
                               v.scalar_tensor_tensor(
                            hs[m2][:, mt, ts2 * 512:(ts2 + 1) * 512], pmm[:, bank:bank + 512],
                            0.0, hs[m2][:, mt, ts2 * 512:(ts2 + 1) * 512],
                            ALU.bypass, ALU.mult),
                            wait=[('MM', mmv)], inc='CP')
                    mm_block(slot, lwv, lambda k, cols=cols: x1f[:, k, cols:cols + 512],
                             [('SC', silu_sc)] if ts2 == 0 else [], 8, 'g', ev_hu)
                ring_war[slot] = ('MM', p.n('MM'))
            h_cp = p.n('CP')
            slots = []
            for m2 in (range(2) if do_shared else []):
                slot, lwv = load_w(wsdv[m2], wait_extra=[('CP', h_cp)] if m2 == 0 else ())
                slots.append((slot, lwv))
            for ts2 in (range(2) if do_shared else []):
                for mt in range(8):
                    bank = (mt % 2) * 512
                    w = [('LW', slots[1][1]), ('CP', h_cp)]
                    if ('g', mt % 2) in pm_war:
                        w.append(pm_war[('g', mt % 2)])
                    for kk in range(16):
                        m2 = kk // 8
                        k = kk % 8
                        s_m2 = slots[m2][0]
                        p.emit('tensor', lambda t, mt=mt, k=k, m2=m2, bank=bank, ts2=ts2,
                               kk=kk, s_m2=s_m2: t.matmul(
                            pmm[:, bank:bank + 512],
                            wring[:, s_m2, k, mt * 128:(mt + 1) * 128],
                            hs[m2][:, k, ts2 * 512:(ts2 + 1) * 512],
                            start=(kk == 0), stop=(kk == 15)),
                            wait=(w if kk == 0 else ()),
                            inc=('MM' if kk == 15 else None))
                    mmv = p.n('MM')
                    p.emit('vector', lambda v, mt=mt, bank=bank, ts2=ts2: v.tensor_tensor(
                        yac[:, mt, ts2 * 512:(ts2 + 1) * 512], pmm[:, bank:bank + 512],
                        yac[:, mt, ts2 * 512:(ts2 + 1) * 512], ALU.add),
                        wait=[('MM', mmv)], inc='CP')
                    pm_war[('g', mt % 2)] = ('CP', p.n('CP'))
            if do_shared:
                ring_war[slots[0][0]] = ('MM', p.n('MM'))
                ring_war[slots[1][0]] = ('MM', p.n('MM'))
            y_cp = p.n('CP')
            for grp in range(2):
                gw = [('CP', y_cp)]
                if x2_war:
                    gw.append(x2_war)
                p.dma('sync', xin[:, :, :], x1v[half * 2 + grp], wait=gw)
                ld_x1 = p.n('LD')
                for i in range(4):
                    tcol = grp * 512 + i * 128
                    twait = [('CP', y_cp)]
                    for par in (0, 1):
                        if ('g', par) in pm_war:
                            twait.append(pm_war[('g', par)])
                    for k in range(8):
                        p.emit('tensor', lambda t, i=i, k=k, tcol=tcol: t.transpose(
                            pmm[:, k * 128:(k + 1) * 128],
                            yac[:, k, tcol:tcol + 128], idnf[:, :]),
                            wait=(twait if k == 0 else ()),
                            inc=('MM' if k == 7 else None))
                    mm_tr = p.n('MM')
                    for k in range(8):
                        p.emit('vector', lambda v, i=i, k=k: v.tensor_tensor(
                            acc[:, i, k * 128:(k + 1) * 128],
                            pmm[:, k * 128:(k + 1) * 128],
                            xin[:, i, k * 128:(k + 1) * 128], ALU.add),
                            wait=([('MM', mm_tr), ('LD', ld_x1)] if k == 0 else ()),
                            inc=('CP' if k == 7 else None))
                    pm_war[('g', 0)] = ('CP', p.n('CP'))
                    pm_war[('g', 1)] = ('CP', p.n('CP'))
                    p.ln(acc[:, i, :], acc[:, i, :], D, stat, (i % 2) * 4, scr[:, :],
                         cst[:, 0:1], wait=[('CP', p.n('CP')), ('GP', gp0)])
                p.dma('sync', x2v[half * 2 + grp], acc[:, :, :],
                      wait=[('CP', p.n('CP')), ('SC', p.n('SC'))], inc='ST')
                st_x2 = p.n('ST')
                x2_war = ('CP', p.n('CP'))
            if dbg and half == 1:
                p.dma('sync', yacd[:, :, :], yac[:, :, :],
                      wait=[('CP', p.n('CP'))], inc='ST')
                st_x2 = p.n('ST')
            yac_war = ('MM', p.n('MM'))
        p.emit('sync', lambda q: q.nop(), wait=[('ST', st_x2)])
        p.finalize(blk)
    return nc


def build_phaseC1():
    nc = bass.Bass("TRN2", target_bir_lowering=False, detect_race_conditions=False)
    x2r = nc.dram_tensor("x2r", [NTOK, D], F32, kind="ExternalInput").ap()
    wqT = nc.dram_tensor("wqT", [D, D], F32, kind="ExternalInput").ap()
    wkT = nc.dram_tensor("wkT", [D, D], F32, kind="ExternalInput").ap()
    wvT = nc.dram_tensor("wvT", [D, D], F32, kind="ExternalInput").ap()
    qbc = nc.dram_tensor("qbc", [128, 8], F32, kind="ExternalInput").ap()
    kbc = nc.dram_tensor("kbc", [128, 8], F32, kind="ExternalInput").ap()
    vbb = nc.dram_tensor("vbb", [128, D], F32, kind="ExternalInput").ap()
    idnd = nc.dram_tensor("idnd", [128, 128], F32, kind="ExternalInput").ap()
    kfo = nc.dram_tensor("kfo", [D, NTOK], F32, kind="ExternalOutput").ap()
    vto = nc.dram_tensor("vto", [NTOK, D], F32, kind="ExternalOutput").ap()
    qfo = nc.dram_tensor("qfo", [D, 512], F32, kind="ExternalOutput").ap()

    NSL = 4
    x2v = _mk_views(x2r)
    wqv = wqT.rearrange("(k p) m -> p k m", k=8, p=128)
    wkv = wkT.rearrange("(k p) m -> p k m", k=8, p=128)
    wvv = wvT.rearrange("(k p) m -> p k m", k=8, p=128)
    vtv = _mk_views(vto)

    from contextlib import ExitStack
    with ExitStack() as _st:
        SB = lambda *a: _st.enter_context(nc.sbuf_tensor(*a))
        PS = lambda *a: _st.enter_context(nc.psum_tensor(*a))
        SEM = lambda n: _st.enter_context(nc.semaphore(n))
        xin = SB("xin", [128, 4, D], F32)
        x2b = SB("x2b", [128, 4, D], BF16)
        x2f = SB("x2f", [128, 8, NTOK], BF16)
        wq = SB("wq", [128, 8, D], BF16)
        wk = SB("wk", [128, 8, D], BF16)
        wv = SB("wv", [128, 8, D], BF16)
        qbcs = SB("qbcs", [128, 8], F32)
        kbcs = SB("kbcs", [128, 8], F32)
        vbbs = SB("vbbs", [128, D], F32)
        kf = SB("kf", [128, 8, NTOK], BF16)
        qf = SB("qf", [128, 8, 512], F32)
        vt = SB("vt", [128, 4, D], F32)
        idnf = SB("idnf", [128, 128], F32)
        idn = SB("idn", [128, 128], BF16)
        ptr = PS("ptr", [128, 2048], BF16)
        pmm = PS("pmm", [128, 1024], F32)
        hands = {s_: SEM(s_) for s_ in SEMS}
        blk = _st.enter_context(nc.Block())
        p = Prog(nc)
        p.handles = hands

        p.dma('sync', idnf[:, :], idnd[:, :])
        p.dma('sync', qbcs[:, :], qbc[:, :])
        p.dma('sync', kbcs[:, :], kbc[:, :])
        p.dma('sync', vbbs[:, :], vbb[:, :])
        ld0 = p.n('LD')
        for wsb, wvw in ((wq, wqv), (wk, wkv), (wv, wvv)):
            p.dma('gpsimd', wsb[:, :, :], wvw, inc='LW')
        lw0 = p.n('LW')
        p.emit('gpsimd', lambda g: g.tensor_copy(idn[:, :], idnf[:, :]),
               wait=[('LD', ld0)], inc='GP')
        gp0 = p.n('GP')

        evac_ms = {}
        prev = None
        for sl in range(NSL):
            wl = list(prev) if prev else []
            p.dma('sync', xin[:, :, :], x2v[sl], wait=wl)
            ld_in = p.n('LD')
            for i in range(4):
                p.emit('gpsimd', lambda g, i=i: g.tensor_copy(x2b[:, i, :], xin[:, i, :]),
                       wait=[('LD', ld_in), ('GP', gp0)], inc='GP')
                gp_cast = p.n('GP')
                half = (i % 2) * 1024
                twait = [('GP', gp_cast)]
                if (i % 2) in evac_ms:
                    twait += [('CP', evac_ms[i % 2])]
                col = sl * 512 + i * 128
                for k in range(8):
                    p.emit('tensor', lambda t, i=i, k=k, half=half: t.transpose(
                        ptr[:, half + k * 128:half + (k + 1) * 128],
                        x2b[:, i, k * 128:(k + 1) * 128], idn[:, :]),
                        wait=(twait if k == 0 else ()),
                        inc=('MM' if k == 7 else None))
                mm_tr = p.n('MM')
                for k in range(8):
                    p.emit('vector', lambda v, k=k, half=half, col=col: v.tensor_copy(
                        x2f[:, k, col:col + 128],
                        ptr[:, half + k * 128:half + (k + 1) * 128]),
                        wait=([('MM', mm_tr)] if k == 0 else ()))
                evac_ms[i % 2] = p.n('CP')
            prev = [('CP', p.n('CP')), ('GP', p.n('GP'))]
        xf_cp = p.n('CP')

        pm_war = {}
        for (wsb, bias, dst, tsl_rng, qmode) in (
                (wk, kbcs, kf, range(4), False), (wq, qbcs, qf, [3], True)):
            for tsl in tsl_rng:
                for mt in range(8):
                    bank = (mt % 2) * 512
                    w = [('CP', xf_cp), ('LW', lw0)]
                    if (mt % 2) in pm_war:
                        w.append(pm_war[mt % 2])
                    for k in range(8):
                        p.emit('tensor', lambda t, mt=mt, k=k, bank=bank, tsl=tsl,
                               wsb=wsb: t.matmul(
                            pmm[:, bank:bank + 512], wsb[:, k, mt * 128:(mt + 1) * 128],
                            x2f[:, k, tsl * 512:(tsl + 1) * 512],
                            start=(k == 0), stop=(k == 7)),
                            wait=(w if k == 0 else ()),
                            inc=('MM' if k == 7 else None))
                    mmv = p.n('MM')
                    if qmode:
                        p.emit('vector', lambda v, mt=mt, bank=bank: v.tensor_scalar(
                            qf[:, mt, :], pmm[:, bank:bank + 512], qbcs[:, mt:mt + 1],
                            None, ALU.add), wait=[('MM', mmv)])
                    else:
                        p.emit('vector', lambda v, mt=mt, bank=bank, tsl=tsl: v.tensor_scalar(
                            kf[:, mt, tsl * 512:(tsl + 1) * 512], pmm[:, bank:bank + 512],
                            kbcs[:, mt:mt + 1], None, ALU.add),
                            wait=[('MM', mmv)])
                    pm_war[mt % 2] = ('CP', p.n('CP'))
        kq_cp = p.n('CP')
        for tt in range(16):
            for nn2 in range(2):
                bank = (nn2 % 2) * 512
                w = [('CP', xf_cp), ('LW', lw0)]
                if ('v', nn2) in pm_war:
                    w.append(pm_war[('v', nn2)])
                elif (nn2 % 2) in pm_war:
                    w.append(pm_war[nn2 % 2])
                for k in range(8):
                    p.emit('tensor', lambda t, tt=tt, nn2=nn2, k=k, bank=bank: t.matmul(
                        pmm[:, bank:bank + 512],
                        x2f[:, k, tt * 128:(tt + 1) * 128],
                        wv[:, k, nn2 * 512:(nn2 + 1) * 512],
                        start=(k == 0), stop=(k == 7)),
                        wait=(w if k == 0 else ()),
                        inc=('MM' if k == 7 else None))
                mmv = p.n('MM')
                vw = [('MM', mmv)]
                if tt % 4 == 0 and tt >= 4:
                    vw.append(('ST', 16 * (tt // 4)))
                p.emit('vector', lambda v, tt=tt, nn2=nn2, bank=bank: v.tensor_tensor(
                    vt[:, tt % 4, nn2 * 512:(nn2 + 1) * 512], pmm[:, bank:bank + 512],
                    vbbs[:, nn2 * 512:(nn2 + 1) * 512], ALU.add),
                    wait=vw)
                pm_war[('v', nn2)] = ('CP', p.n('CP'))
            if tt % 4 == 3:
                p.dma('sync', vtv[tt // 4], vt[:, :, :],
                      wait=[('CP', p.n('CP'))], inc='ST')
        p.dma('gpsimd', kfo.rearrange("(k p) n -> p k n", k=8, p=128), kf[:, :, :],
              wait=[('CP', kq_cp)], inc='ST')
        p.dma('gpsimd', qfo.rearrange("(k p) n -> p k n", k=8, p=128), qf[:, :, :],
              wait=[('CP', kq_cp)], inc='ST')
        p.emit('sync', lambda q: q.nop(), wait=[('ST', p.n('ST'))])
        p.finalize(blk)
    return nc


def build_phaseC2():
    nc = bass.Bass("TRN2", target_bir_lowering=False, detect_race_conditions=False)
    khTd = nc.dram_tensor("khTd", [512, 4096], F32, kind="ExternalInput").ap()
    vsld = nc.dram_tensor("vsld", [4096, 512], F32, kind="ExternalInput").ap()
    qhTd = nc.dram_tensor("qhTd", [512, 512], F32, kind="ExternalInput").ap()
    mkCd = nc.dram_tensor("mkCd", [128, 2048], F32, kind="ExternalInput").ap()
    mkTd = nc.dram_tensor("mkTd", [128, 2048], F32, kind="ExternalInput").ap()
    ood = nc.dram_tensor("ood", [512, 512], F32, kind="ExternalOutput").ap()

    from contextlib import ExitStack
    with ExitStack() as _st:
        SB = lambda *a: _st.enter_context(nc.sbuf_tensor(*a))
        PS = lambda *a: _st.enter_context(nc.psum_tensor(*a))
        SEM = lambda n: _st.enter_context(nc.semaphore(n))
        kh = SB("kh", [128, 4, 4096], BF16)
        vt = SB("vt", [128, 32, 512], BF16)
        qh = SB("qh", [128, 4, 512], BF16)
        mkC = SB("mkC", [128, 4, 512], F32)
        mkT = SB("mkT", [128, 4, 512], F32)
        ssb = SB("ssb", [128, 2, 512], F32)
        scr = SB("scr", [128, 512], F32)
        Ssum = SB("Ssum", [128, 4, 8], F32)
        mx = SB("mx", [128, 4, 8], F32)
        mrun = SB("mrun", [128, 4, 8], F32)
        Lc = SB("Lc", [128, 4, 8], F32)
        rc = SB("rc", [128, 4, 8], F32)
        rr = SB("rr", [128, 4, 8], F32)
        omr = SB("omr", [128, 4, 8], F32)
        tv = SB("tv", [128, 4, 8], F32)
        dd = SB("dd", [128, 4, 8], F32)
        em = SB("em", [128, 4, 8], F32)
        dm7 = SB("dm7", [128, 4, 7], F32)
        ninf = SB("ninf", [128, 8], F32)
        osb = SB("osb", [128, 8, 4, 64], F32)
        ptb = SB("ptb", [128, 2, 4, 512], BF16)
        pss = PS("pss", [128, 1024], F32)
        pst = PS("pst", [128, 1024], F32)
        pob = PS("pob", [128, 256], F32)
        hands = {s_: SEM(s_) for s_ in SEMS}
        blk = _st.enter_context(nc.Block())
        p = Prog(nc)
        p.handles = hands

        p.dma('gpsimd', kh[:, :, :], khTd.rearrange("(g p) n -> p g n", g=4, p=128), inc='LW')
        p.dma('gpsimd', vt[:, :, :], vsld.rearrange("(c p) n -> p c n", c=32, p=128), inc='LW')
        p.dma('gpsimd', qh[:, :, :], qhTd.rearrange("(g p) n -> p g n", g=4, p=128), inc='LW')
        lw0 = p.n('LW')
        p.dma('sync', mkC[:, :, :], mkCd.rearrange("p (g n) -> p g n", g=4, n=512))
        p.dma('sync', mkT[:, :, :], mkTd.rearrange("p (g n) -> p g n", g=4, n=512))
        ld0 = p.n('LD')
        p.emit('gpsimd', lambda g: g.memset(osb[:, :, :, :], 0.0))
        p.emit('gpsimd', lambda g: g.memset(ninf[:, :], -3.0e38), inc='GP')
        gp0 = p.n('GP')

        pm_war = {}
        for h in range(8):
            hg, hb = h // 2, (h % 2) * 64
            ssb_war = {}
            for j in range(8):
                for qs in range(4):
                    par = (j * 4 + qs) % 2
                    bank = par * 512
                    w = [('LW', lw0), ('GP', gp0)]
                    if ('s', par) in pm_war:
                        w.append(pm_war[('s', par)])
                    p.emit('tensor', lambda t, h=h, hg=hg, hb=hb, j=j, qs=qs, bank=bank:
                           t.matmul(
                        pss[:, bank:bank + 512],
                        qh[hb:hb + 64, hg, qs * 128:(qs + 1) * 128],
                        kh[hb:hb + 64, hg, j * 512:(j + 1) * 512],
                        start=True, stop=True),
                        wait=w, inc='MM')
                    mmv = p.n('MM')
                    evw = [('MM', mmv), ('LD', ld0)]
                    if par in ssb_war:
                        evw += ssb_war[par]
                    if j == 7:
                        p.emit('vector', lambda v, qs=qs, par=par, bank=bank:
                               v.scalar_tensor_tensor(
                            ssb[:, par, :], pss[:, bank:bank + 512], 0.125,
                            mkC[:, qs, :], ALU.mult, ALU.add), wait=evw)
                    else:
                        p.emit('vector', lambda v, par=par, bank=bank: v.tensor_scalar(
                            ssb[:, par, :], pss[:, bank:bank + 512], 0.125, None,
                            ALU.mult), wait=evw)
                    pm_war[('s', par)] = ('CP', p.n('CP'))
                    cpe = p.n('CP')
                    p.emit('vector', lambda v, j=j, qs=qs, par=par: v.tensor_reduce(
                        mx[:, qs, j:j + 1], ssb[:, par, :], AX.X, ALU.max),
                        wait=[('CP', cpe)])
                    p.emit('scalar', lambda s, j=j, qs=qs, par=par: s.activation(
                        scr[:, :], ssb[:, par, :], AF.Exp,
                        accum_out=Ssum[:, qs, j:j + 1]),
                        wait=[('CP', cpe)], inc='SC')
                    ssb_war[par] = [('CP', p.n('CP')), ('SC', p.n('SC'))]
            st_sc, st_gp = p.n('SC'), p.n('CP')
            for qs in range(4):
                p.emit('vector', lambda v, qs=qs: v.tensor_tensor_scan(
                    mrun[:, qs, :], mx[:, qs, :], mx[:, qs, :], -3.0e38,
                    ALU.max, ALU.max), wait=[('SC', st_sc), ('CP', st_gp)])
                p.emit('vector', lambda v, qs=qs: v.tensor_tensor_scan(
                    Lc[:, qs, :], Ssum[:, qs, :], ninf[:, :], 0.0,
                    ALU.add, ALU.max))
            p.emit('vector', lambda v: v.reciprocal(
                rc[:, :, :].rearrange("p a b -> p (a b)"),
                Lc[:, :, :].rearrange("p a b -> p (a b)")))
            p.emit('gpsimd', lambda g: g.memset(rr[:, :, 0:1], 0.0), inc='GP')
            gprr = p.n('GP')
            p.emit('vector', lambda v: v.tensor_tensor(
                rr[:, :, 1:8], Lc[:, :, 0:7], rc[:, :, 1:8], ALU.mult),
                wait=[('GP', gprr)])
            p.emit('vector', lambda v: v.tensor_tensor(
                dm7[:, :, :], mrun[:, :, 1:8], mrun[:, :, 0:7], ALU.subtract), inc='CP')
            p.emit('scalar', lambda s: s.activation(
                dm7[:, :, :], dm7[:, :, :], AF.Exp, scale=-1.0),
                wait=[('CP', p.n('CP'))], inc='SC')
            p.emit('scalar', lambda s: s.activation(
                em[:, :, :], mrun[:, :, :], AF.Exp, scale=-1.0), inc='SC')
            p.emit('vector', lambda v: v.tensor_tensor(
                rr[:, :, 1:8], rr[:, :, 1:8], dm7[:, :, :], ALU.mult),
                wait=[('SC', p.n('SC'))])
            p.emit('vector', lambda v: v.tensor_scalar(
                omr[:, :, :], rr[:, :, :], -1.0, 1.0, ALU.mult, ALU.add))
            p.emit('gpsimd', lambda g: g.memset(tv[:, :, 7:8], 1.0), inc='GP')
            gptv = p.n('GP')
            first = True
            for j in range(6, -1, -1):
                p.emit('vector', lambda v, j=j: v.tensor_tensor(
                    tv[:, :, j:j + 1], tv[:, :, j + 1:j + 2], omr[:, :, j + 1:j + 2],
                    ALU.mult), wait=([('GP', gptv)] if first else ()))
                first = False
            p.emit('vector', lambda v: v.tensor_tensor(
                dd[:, :, :], rr[:, :, :], tv[:, :, :], ALU.mult))
            p.emit('vector', lambda v: v.tensor_tensor(
                dd[:, :, :], dd[:, :, :], em[:, :, :], ALU.mult))
            p.emit('vector', lambda v: v.tensor_tensor(
                rc[:, :, 0:1], Lc[:, :, 7:8], em[:, :, 7:8], ALU.mult))
            p.emit('vector', lambda v: v.reciprocal(
                rc[:, :, 1:2].rearrange("p a b -> p (a b)"),
                rc[:, :, 0:1].rearrange("p a b -> p (a b)")))
            p.emit('vector', lambda v: v.tensor_tensor(
                dd[:, :, :], dd[:, :, :],
                rc[:, :, 1:2].broadcast_to([128, 4, 8]), ALU.mult), inc='CP')
            dd_cp = p.n('CP')
            for j in range(1, 8):
                par = j % 2
                for kc in range(4):
                    bank = (kc % 2) * 512
                    w = [('LW', lw0)]
                    if ('t', kc % 2) in pm_war:
                        w.append(pm_war[('t', kc % 2)])
                    p.emit('tensor', lambda t, hg=hg, hb=hb, j=j, kc=kc, bank=bank:
                           t.matmul(
                        pst[:, bank:bank + 512],
                        kh[hb:hb + 64, hg, j * 512 + kc * 128:j * 512 + (kc + 1) * 128],
                        qh[hb:hb + 64, hg, :],
                        start=True, stop=True), wait=w, inc='MM')
                    mmv = p.n('MM')
                    evw = [('MM', mmv)]
                    if ('pt', par, kc) in pm_war:
                        evw.append(pm_war[('pt', par, kc)])
                    if j == 7:
                        p.emit('vector', lambda v, kc=kc, bank=bank: v.scalar_tensor_tensor(
                            scr[:, :], pst[:, bank:bank + 512], 0.125,
                            mkT[:, kc, :], ALU.mult, ALU.add), wait=evw)
                    else:
                        p.emit('vector', lambda v, kc=kc, bank=bank: v.tensor_scalar(
                            scr[:, :], pst[:, bank:bank + 512], 0.125, None,
                            ALU.mult), wait=evw)
                    pm_war[('t', kc % 2)] = ('CP', p.n('CP'))
                    p.emit('scalar', lambda s, par=par, kc=kc: s.activation(
                        ptb[:, par, kc, :], scr[:, :], AF.Exp),
                        wait=[('CP', p.n('CP'))], inc='SC')
                    pm_war[('t', kc % 2)] = ('SC', p.n('SC'))
                sc_pt = p.n('SC')
                for qs in range(4):
                    w = [('SC', sc_pt), ('CP', dd_cp)]
                    if ('ob', 0) in pm_war:
                        w.append(pm_war[('ob', 0)])
                    for kc in range(4):
                        p.emit('tensor', lambda t, h=h, j=j, qs=qs, kc=kc, par=par:
                               t.matmul(
                            pob[:, qs * 64:(qs + 1) * 64],
                            ptb[:, par, kc, qs * 128:(qs + 1) * 128],
                            vt[:, j * 4 + kc, h * 64:(h + 1) * 64],
                            start=(kc == 0), stop=(kc == 3)),
                            wait=(w if kc == 0 else ()),
                            inc=('MM' if kc == 3 else None))
                mmv = p.n('MM')
                for qs in range(4):
                    p.emit('vector', lambda v, h=h, j=j, qs=qs: v.scalar_tensor_tensor(
                        osb[:, h, qs, :], pob[:, qs * 64:(qs + 1) * 64],
                        dd[:, qs, j:j + 1], osb[:, h, qs, :],
                        ALU.mult, ALU.add),
                        wait=([('MM', mmv)] if qs == 0 else ()))
                pm_war[('ob', 0)] = ('CP', p.n('CP'))
        o_cp = p.n('CP')
        oodv = ood.rearrange("(qs p) f -> p qs f", qs=4, p=128)
        for h in range(8):
            p.dma('sync', oodv[:, :, h * 64:(h + 1) * 64], osb[:, h, :, :],
                  wait=([('CP', o_cp)] if h == 0 else ()), inc='ST')
        p.emit('sync', lambda q: q.nop(), wait=[('ST', p.n('ST'))])
        p.finalize(blk)
    return nc


def build_phaseC3():
    nc = bass.Bass("TRN2", target_bir_lowering=False, detect_race_conditions=False)
    orows = nc.dram_tensor("orows", [256, D], F32, kind="ExternalInput").ap()
    pwT = nc.dram_tensor("pwT", [D, D], F32, kind="ExternalInput").ap()
    idnd = nc.dram_tensor("idnd", [128, 128], F32, kind="ExternalInput").ap()
    apro = nc.dram_tensor("apro", [256, D], F32, kind="ExternalOutput").ap()
    ov = orows.rearrange("(i p) f -> p i f", i=2, p=128)
    av = apro.rearrange("(i p) f -> p i f", i=2, p=128)
    pwv = pwT.rearrange("(k p) m -> p k m", k=8, p=128)
    from contextlib import ExitStack
    with ExitStack() as _st:
        SB = lambda *a: _st.enter_context(nc.sbuf_tensor(*a))
        PS = lambda *a: _st.enter_context(nc.psum_tensor(*a))
        SEM = lambda n: _st.enter_context(nc.semaphore(n))
        xin = SB("xin", [128, 2, D], F32)
        xb = SB("xb", [128, 2, D], BF16)
        xf = SB("xf", [128, 8, 256], BF16)
        wsb = SB("wsb", [128, 8, D], BF16)
        oo = SB("oo", [128, 2, D], F32)
        idnf = SB("idnf", [128, 128], F32)
        idn = SB("idn", [128, 128], BF16)
        ptr = PS("ptr", [128, 2048], BF16)
        pmm = PS("pmm", [128, 1024], F32)
        hands = {s_: SEM(s_) for s_ in SEMS}
        blk = _st.enter_context(nc.Block())
        p = Prog(nc)
        p.handles = hands
        p.dma('sync', idnf[:, :], idnd[:, :])
        p.dma('sync', xin[:, :, :], ov)
        ld0 = p.n('LD')
        p.dma('gpsimd', wsb[:, :, :], pwv, inc='LW')
        lw0 = p.n('LW')
        p.emit('gpsimd', lambda g: g.tensor_copy(idn[:, :], idnf[:, :]),
               wait=[('LD', ld0)])
        p.emit('gpsimd', lambda g: g.tensor_copy(xb[:, :, :], xin[:, :, :]), inc='GP')
        gp0 = p.n('GP')
        for i in range(2):
            for k in range(8):
                p.emit('tensor', lambda t, i=i, k=k: t.transpose(
                    ptr[:, i * 1024 + k * 128:i * 1024 + (k + 1) * 128],
                    xb[:, i, k * 128:(k + 1) * 128], idn[:, :]),
                    wait=([('GP', gp0)] if (i == 0 and k == 0) else ()),
                    inc=('MM' if k == 7 else None))
            mm_tr = p.n('MM')
            for k in range(8):
                p.emit('vector', lambda v, i=i, k=k: v.tensor_copy(
                    xf[:, k, i * 128:(i + 1) * 128],
                    ptr[:, i * 1024 + k * 128:i * 1024 + (k + 1) * 128]),
                    wait=([('MM', mm_tr)] if k == 0 else ()))
        xf_cp = p.n('CP')
        for i in range(2):
            for nn2 in range(2):
                bank = nn2 * 512
                for k in range(8):
                    p.emit('tensor', lambda t, i=i, nn2=nn2, k=k, bank=bank: t.matmul(
                        pmm[:, bank:bank + 512],
                        xf[:, k, i * 128:(i + 1) * 128],
                        wsb[:, k, nn2 * 512:(nn2 + 1) * 512],
                        start=(k == 0), stop=(k == 7)),
                        wait=([('CP', xf_cp), ('LW', lw0)] if k == 0 else ()),
                        inc=('MM' if k == 7 else None))
                mmv = p.n('MM')
                p.emit('vector', lambda v, i=i, nn2=nn2, bank=bank: v.tensor_copy(
                    oo[:, i, nn2 * 512:(nn2 + 1) * 512], pmm[:, bank:bank + 512]),
                    wait=[('MM', mmv)])
        p.dma('sync', av, oo[:, :, :], wait=[('CP', p.n('CP'))], inc='ST')
        p.emit('sync', lambda q: q.nop(), wait=[('ST', p.n('ST'))])
        p.finalize(blk)
    return nc


def build_phaseA(ngroups=16):
    nc = bass.Bass("TRN2", target_bir_lowering=False, detect_race_conditions=False)
    xTd = nc.dram_tensor("xTd", [D, 4096], F32, kind="ExternalInput").ap()
    wind = nc.dram_tensor("wind", [D, 4256], F32, kind="ExternalInput").ap()
    convwd = nc.dram_tensor("convwd", [128, 68], F32, kind="ExternalInput").ap()
    convbd = nc.dram_tensor("convbd", [128, 17], F32, kind="ExternalInput").ap()
    dtbd = nc.dram_tensor("dtbd", [128, 32], F32, kind="ExternalInput").ap()
    alogd = nc.dram_tensor("alogd", [1, 32], F32, kind="ExternalInput").ap()
    dskcd = nc.dram_tensor("dskcd", [128, 32], F32, kind="ExternalInput").ap()
    nwd = nc.dram_tensor("nwd", [128, 2048], F32, kind="ExternalInput").ap()
    woutd = nc.dram_tensor("woutd", [2048, D], F32, kind="ExternalInput").ap()
    wohd = nc.dram_tensor("wohd", [D, D], F32, kind="ExternalInput").ap()
    idnd = nc.dram_tensor("idnd", [128, 128], F32, kind="ExternalInput").ap()
    incUd = nc.dram_tensor("incUd", [128, 128], F32, kind="ExternalInput").ap()
    triLd = nc.dram_tensor("triLd", [128, 128], F32, kind="ExternalInput").ap()
    prowd = nc.dram_tensor("prowd", [4096, D], F32, kind="ExternalOutput").ap()
    csscr = nc.dram_tensor("csscr", [32, 128], F32, kind="Internal").ap()

    NG = ngroups
    NCH = 2
    xTv = xTd.rearrange("(k p) (g n) -> g p k n", k=8, p=128, g=16, n=256)
    winv = wind.rearrange("(k p) m -> p k m", k=8, p=128)
    woutv = woutd.rearrange("(kh k p) m -> kh p k m", kh=2, k=8, p=128)
    wohv = wohd.rearrange("(k p) n -> p k n", k=8, p=128)
    prv = prowd.rearrange("(g i p) f -> g p i f", g=16, i=2, p=128)

    from contextlib import ExitStack
    with ExitStack() as _st:
        SB = lambda *a: _st.enter_context(nc.sbuf_tensor(*a))
        PS = lambda *a: _st.enter_context(nc.psum_tensor(*a))
        SEM = lambda n: _st.enter_context(nc.semaphore(n))
        xT = SB("xT", [128, 2, 8, 256], BF16)
        win = SB("win", [128, 8, 4256], BF16)
        wouth = SB("wouth", [128, 8, D], BF16)
        woh = SB("woh", [128, 8, D], BF16)
        convw = SB("convw", [128, 17, 4], F32)
        convb = SB("convb", [128, 17], F32)
        dtbb = SB("dtbb", [128, 32], F32)
        alogr = SB("alogr", [1, 32], F32)
        narow = SB("narow", [1, 32], F32)
        nab = SB("nab", [128, 32], F32)
        ones1 = SB("ones1", [1, 128], F32)
        dskc = SB("dskc", [128, 32], F32)
        nwb = SB("nwb", [128, 2048], BF16)
        idnf = SB("idnf", [128, 128], F32)
        idn = SB("idn", [128, 128], BF16)
        incU = SB("incU", [128, 128], F32)
        triL = SB("triL", [128, 128], F32)
        xbc = SB("xbc", [128, 17, 260], BF16)
        xbcs = SB("xbcs", [128, 17, 256], BF16)
        cacv = SB("cacv", [128, 256], F32)
        cacg = SB("cacg", [128, 256], F32)
        cacg2 = SB("cacg2", [128, 256], F32)
        zt = SB("zt", [128, 2048], BF16)
        dts = SB("dts", [128, 2, 32], F32)
        csx = SB("csx", [128, 2, 32], F32)
        bcol = SB("bcol", [128, 2, 32], F32)
        lamT = SB("lamT", [128, 2, 32], F32)
        tmpd = SB("tmpd", [128, 2, 32], F32)
        tmpd2 = SB("tmpd2", [128, 2, 32], F32)
        csfs = SB("csfs", [32, 256], F32)
        strip = SB("strip", [1, 2048], F32)
        lrow = SB("lrow", [1, 32], F32)
        lc64 = SB("lc64", [64, 32], F32)
        Egrp = SB("Egrp", [128, 8, 512], BF16)
        GTm = SB("GTm", [128, 128], BF16)
        Ecols = SB("Ecols", [128, 32], F32)
        xst = SB("xst", [128, 16, 128], BF16)
        btm = SB("btm", [128, 64], BF16)
        ctm = SB("ctm", [128, 64], BF16)
        cf0 = SB("cf0", [64, 128], BF16)
        bw = SB("bw", [128, 32, 64], BF16)
        hT = SB("hT", [64, 32, 64], F32)
        hTb = SB("hTb", [64, 32, 64], BF16)
        Ysb = SB("Ysb", [128, 2048], F32)
        yscr = SB("yscr", [128, 1024], F32)
        ynm = SB("ynm", [128, 2048], BF16)
        ytf = SB("ytf", [128, 16, 256], BF16)
        ppre = SB("ppre", [128, 8, 256], BF16)
        stat = SB("stat", [128, 8], F32)
        p1 = PS("p1", [128, 1024], F32)
        p2 = PS("p2", [128, 1024], F32)
        p3 = PS("p3", [128, 1024], F32)
        p4 = PS("p4", [128, 2048], BF16)
        hands = {s_: SEM(s_) for s_ in SEMS}
        blk = _st.enter_context(nc.Block())
        p = Prog(nc)
        p.handles = hands

        for dst, srcv in ((convw[:, :, :].rearrange("p a b -> p (a b)"), convwd[:, :]),
                          (convb[:, :], convbd[:, :]),
                          (dtbb[:, :], dtbd[:, :]), (alogr[:, :], alogd[:, :]),
                          (idnf[:, :], idnd[:, :]), (incU[:, :], incUd[:, :]),
                          (triL[:, :], triLd[:, :]), (dskc[:, :], dskcd[:, :])):
            p.dma('sync', dst, srcv)
        ld0 = p.n('LD')
        p.dma('gpsimd', nwb[:, :], nwd[:, :], inc='LW')
        p.dma('gpsimd', win[:, :, :], winv, inc='LW')
        p.dma('gpsimd', woh[:, :, :], wohv, inc='LW')
        lw0 = p.n('LW')
        p.emit('gpsimd', lambda g: g.tensor_copy(idn[:, :], idnf[:, :]),
               wait=[('LD', ld0)])
        p.emit('gpsimd', lambda g: g.memset(ones1[:, :], 1.0))
        p.emit('gpsimd', lambda g: g.memset(xbc[:, :, 256:260], 0.0))
        p.emit('gpsimd', lambda g: g.memset(hT[:, :, :], 0.0))
        p.emit('gpsimd', lambda g: g.memset(hTb[:, :, :], 0.0))
        p.emit('gpsimd', lambda g: g.memset(stat[:, 4:5], LNEPS), inc='GP')
        gp0 = p.n('GP')
        p.emit('scalar', lambda s: s.activation(narow[:, :], alogr[:, :], AF.Exp),
               wait=[('LD', ld0)], inc='SC')
        p.emit('vector', lambda v: v.tensor_scalar(narow[:, :], narow[:, :], -1.0,
                                                   None, ALU.mult),
               wait=[('SC', p.n('SC'))])
        p.emit('tensor', lambda t: t.matmul(p2[:, 0:32], ones1[0:1, :], narow[0:1, :],
                                            start=True, stop=True),
               wait=[('CP', p.n('CP')), ('GP', gp0)], inc='MM')
        p.emit('vector', lambda v: v.tensor_copy(nab[:, :], p2[:, 0:32]),
               wait=[('MM', p.n('MM'))])
        na_cp = p.n('CP')

        pm = {}
        xT_war = {}
        wouth_war = None
        xbc_war = None
        xst_war = None
        conv_sc_prev = {0: None, 1: None}
        for g in range(NG):
            slot = g % 2
            wl = [xT_war[slot]] if slot in xT_war else []
            p.dma('gpsimd', xT[:, slot, :, :], xTv[g], wait=wl, inc='LW')
            lwx = p.n('LW')
            cw = [('GP', gp0), ('CP', na_cp)]
            if xbc_war:
                cw.append(xbc_war)
            for ct in range(17):
                p.emit('vector', lambda v, ct=ct: v.tensor_copy(
                    xbc[:, ct, 0:4], xbc[:, ct, 256:260]),
                    wait=(cw if ct == 0 else ()))
            carry_cp = p.n('CP')
            for ct in range(17):
                par = ct % 2
                bank = par * 512
                w = [('LW', lw0), ('LW', lwx), ('CP', carry_cp)]
                if ('p1', par) in pm:
                    w.append(pm[('p1', par)])
                for k in range(8):
                    p.emit('tensor', lambda t, ct=ct, k=k, bank=bank, slot=slot: t.matmul(
                        p1[:, bank:bank + 256],
                        win[:, k, 2048 + ct * 128:2048 + (ct + 1) * 128],
                        xT[:, slot, k, :],
                        start=(k == 0), stop=(k == 7)),
                        wait=(w if k == 0 else ()),
                        inc=('MM' if k == 7 else None))
                mmv = p.n('MM')
                p.emit('vector', lambda v, ct=ct, bank=bank: v.tensor_copy(
                    xbc[:, ct, 4:260], p1[:, bank:bank + 256]),
                    wait=[('MM', mmv)])
                pm[('p1', par)] = ('CP', p.n('CP'))
            xbcr_cp = p.n('CP')
            for ct in range(17):
                is_v = (ct % 2 == 0)
                eng = 'vector' if is_v else 'gpsimd'
                acc = cacv if is_v else cacg
                w0 = []
                if not is_v:
                    w0.append(('CP', xbcr_cp))
                prevsc = conv_sc_prev[ct % 2]
                if prevsc is not None:
                    w0.append(prevsc)
                if is_v:
                    p.emit(eng, lambda E, ct=ct, acc=acc: E.tensor_scalar(
                        acc[:, :], xbc[:, ct, 1:257], convw[:, ct, 0:1], None, ALU.mult),
                        wait=w0)
                    for k in range(1, 4):
                        p.emit(eng, lambda E, ct=ct, k=k, acc=acc: E.scalar_tensor_tensor(
                            acc[:, :], xbc[:, ct, 1 + k:257 + k], convw[:, ct, k:k + 1],
                            acc[:, :], ALU.mult, ALU.add))
                else:
                    p.emit(eng, lambda E, ct=ct, acc=acc: E.tensor_tensor(
                        acc[:, :], xbc[:, ct, 1:257],
                        convw[:, ct, 0:1].broadcast_to([128, 256]), ALU.mult),
                        wait=w0, inc='GP')
                    for k in range(1, 4):
                        p.emit(eng, lambda E, ct=ct, k=k: E.tensor_tensor(
                            cacg2[:, :], xbc[:, ct, 1 + k:257 + k],
                            convw[:, ct, k:k + 1].broadcast_to([128, 256]), ALU.mult),
                            inc='GP')
                        p.emit(eng, lambda E, acc=acc: E.tensor_tensor(
                            acc[:, :], acc[:, :], cacg2[:, :], ALU.add), inc='GP')
                mark = [('CP', p.n('CP'))] if is_v else [('GP', p.n('GP'))]
                p.emit('scalar', lambda s, ct=ct, acc=acc: s.activation(
                    xbcs[:, ct, :], acc[:, :], AF.Silu, bias=convb[:, ct:ct + 1]),
                    wait=mark, inc='SC')
                conv_sc_prev[ct % 2] = ('SC', p.n('SC'))
            conv_sc = p.n('SC')
            xbc_war = ('SC', conv_sc)
            w = [('LW', lwx)]
            if ('p2a',) in pm:
                w.append(pm[('p2a',)])
            for c4 in range(NCH):
                for k in range(8):
                    p.emit('tensor', lambda t, c4=c4, k=k, slot=slot: t.matmul(
                        p2[:, 128 + c4 * 32:128 + (c4 + 1) * 32],
                        xT[:, slot, k, c4 * 128:(c4 + 1) * 128],
                        win[:, k, 4224:4256],
                        start=(k == 0), stop=(k == 7)),
                        wait=(w if (c4 == 0 and k == 0) else ()),
                        inc=('MM' if k == 7 else None))
            mmv = p.n('MM')
            p.emit('vector', lambda v: v.tensor_tensor(
                dts[:, :, :],
                p2[:, 128:192].rearrange("p (c e) -> p c e", c=2, e=32),
                dtbb[:, :].unsqueeze(1).broadcast_to([128, 2, 32]), ALU.add),
                wait=[('MM', mmv)])
            pm[('p2a',)] = ('CP', p.n('CP'))
            dta_cp = p.n('CP')
            p.emit('scalar', lambda s: s.activation(
                tmpd[:, :, :], dts[:, :, :], AF.Abs), wait=[('CP', dta_cp)])
            p.emit('scalar', lambda s: s.activation(
                tmpd[:, :, :], tmpd[:, :, :], AF.Exp, scale=-1.0))
            p.emit('scalar', lambda s: s.activation(
                tmpd[:, :, :], tmpd[:, :, :], AF.Ln, bias=1.0), inc='SC')
            p.emit('vector', lambda v: v.tensor_scalar(
                tmpd2[:, :, :], dts[:, :, :], 0.0, None, ALU.max),
                wait=[('SC', p.n('SC'))])
            p.emit('vector', lambda v: v.tensor_tensor(
                dts[:, :, :], tmpd2[:, :, :], tmpd[:, :, :], ALU.add))
            p.emit('vector', lambda v: v.tensor_tensor(
                tmpd[:, :, :], dts[:, :, :],
                nab[:, :].unsqueeze(1).broadcast_to([128, 2, 32]), ALU.mult), inc='CP')
            al_cp = p.n('CP')
            w2w = [('CP', al_cp), ('LD', ld0)]
            if ('p2b',) in pm:
                w2w.append(pm[('p2b',)])
            for c4 in range(NCH):
                p.emit('tensor', lambda t, c4=c4: t.matmul(
                    p2[:, 256 + c4 * 32:256 + (c4 + 1) * 32], incU[:, :],
                    tmpd[:, c4, :], start=True, stop=True),
                    wait=(w2w if c4 == 0 else ()), inc='MM')
            mmv = p.n('MM')
            p.emit('vector', lambda v: v.tensor_copy(
                csx[:, :, :], p2[:, 256:320].rearrange("p (c e) -> p c e", c=2, e=32)),
                wait=[('MM', mmv)])
            pm[('p2b',)] = ('CP', p.n('CP'))
            p.emit('scalar', lambda s: s.activation(
                tmpd2[:, :, :], dts[:, :, :], AF.Ln), wait=[('CP', p.n('CP'))])
            p.emit('scalar', lambda s: s.activation(
                lamT[:, :, :], csx[:, :, :], AF.Exp), inc='SC')
            p.emit('vector', lambda v: v.tensor_tensor(
                bcol[:, :, :], tmpd2[:, :, :], csx[:, :, :], ALU.subtract),
                wait=[('SC', p.n('SC'))], inc='CP')
            bc_cp = p.n('CP')

            for c4 in range(NCH):
                w3 = [('CP', bc_cp)]
                if ('p3c',) in pm:
                    w3.append(pm[('p3c',)])
                p.emit('tensor', lambda t, c4=c4: t.transpose(
                    p3[0:32, 0:128], csx[:, c4, :], idnf[:, :]),
                    wait=w3, inc='MM')
                p.emit('vector', lambda v: v.tensor_copy(csfs[:, 0:128], p3[0:32, 0:128]),
                       wait=[('MM', p.n('MM'))])
                pm[('p3c',)] = ('CP', p.n('CP'))
                p.dma('sync', csscr[:, :], csfs[:, 0:128],
                      wait=[('CP', p.n('CP'))], inc='ST')
                p.dma('sync', strip[:, :],
                      csscr[0:16, :].rearrange("h t -> (h t)").unsqueeze(0))
                ld_strip = p.n('LD')
                for g4 in range(8):
                    if g4 == 0:
                        p.emit('scalar', lambda s: s.activation(
                            lrow[0:1, 0:16],
                            strip.rearrange("o (h t) -> o h t", h=16, t=128)[:, :, 127],
                            AF.Exp), wait=[('LD', ld_strip)], inc='SC')
                    if g4 == 4:
                        p.dma('sync', strip[:, :],
                              csscr[16:32, :].rearrange("h t -> (h t)").unsqueeze(0),
                              wait=[('MM', p.n('MM')), ('SC', p.n('SC'))])
                        ld_strip = p.n('LD')
                        p.emit('scalar', lambda s: s.activation(
                            lrow[0:1, 16:32],
                            strip.rearrange("o (h t) -> o h t", h=16, t=128)[:, :, 127],
                            AF.Exp), wait=[('LD', ld_strip)], inc='SC')
                    w = [('LD', ld_strip)]
                    if ('p2d',) in pm:
                        w.append(pm[('p2d',)])
                    p.emit('tensor', lambda t, g4=g4: t.matmul(
                        p2[:, 512:1024], ones1[0:1, :],
                        strip[0:1, (g4 % 4) * 512:(g4 % 4 + 1) * 512],
                        start=True, stop=True), wait=w, inc='MM')
                    mmv = p.n('MM')
                    p.emit('vector', lambda v: v.tensor_tensor(
                        yscr[:, 0:512].rearrange("p (a b) -> p a b", a=4, b=128),
                        p2[:, 512:1024].rearrange("p (a b) -> p a b", a=4, b=128),
                        triL[:, :].unsqueeze(1).broadcast_to([128, 4, 128]),
                        ALU.add), wait=[('MM', mmv)])
                    pm[('p2d',)] = ('CP', p.n('CP'))
                    dm_cp = p.n('CP')
                    for i in range(4):
                        h = g4 * 4 + i
                        p.emit('scalar', lambda s, g4=g4, i=i, h=h, c4=c4: s.activation(
                            Egrp[:, g4, i * 128:(i + 1) * 128],
                            yscr[:, i * 128:(i + 1) * 128], AF.Exp,
                            bias=bcol[:, c4, h:h + 1]),
                            wait=[('CP', dm_cp)],
                            inc=('SC' if i == 3 else None))
                    pm[('p2d',)] = ('SC', p.n('SC'))
                eg_sc = p.n('SC')
                for g4 in range(8):
                    p.emit('vector', lambda v, g4=g4: v.tensor_copy(
                        Ecols[:, g4 * 4:(g4 + 1) * 4],
                        Egrp[:, g4, :].rearrange("p (a b) -> p a b", a=4, b=128)[:, :, 127]),
                        wait=([('SC', eg_sc)] if g4 == 0 else ()))
                ec_cp = p.n('CP')
                w = [('SC', conv_sc)]
                if ('p3g',) in pm:
                    w.append(pm[('p3g',)])
                p.emit('tensor', lambda t, c4=c4: t.transpose(
                    p4[:, 1920:1984], xbcs[64:128, 16, c4 * 128:(c4 + 1) * 128],
                    idn[64:128, 64:128]), wait=w, inc='MM')
                p.emit('vector', lambda v: v.tensor_copy(ctm[:, :], p4[:, 1920:1984]),
                       wait=[('MM', p.n('MM'))], inc='CP')
                p.emit('tensor', lambda t: t.transpose(
                    p4[0:64, 1792:1920], ctm[:, :], idn[:, :]),
                    wait=[('CP', p.n('CP'))], inc='MM')
                p.emit('vector', lambda v: v.tensor_copy(cf0[:, :], p4[0:64, 1792:1920]),
                       wait=[('MM', p.n('MM'))], inc='CP')
                cf_cp = p.n('CP')
                p.emit('tensor', lambda t, c4=c4: t.matmul(
                    p3[:, 0:128], xbcs[0:64, 16, c4 * 128:(c4 + 1) * 128],
                    cf0[:, :],
                    start=True, stop=True), wait=[('CP', cf_cp)], inc='MM')
                p.emit('vector', lambda v: v.tensor_copy(GTm[:, :], p3[:, 0:128]),
                       wait=[('MM', p.n('MM'))])
                pm[('p3g',)] = ('CP', p.n('CP'))
                for g4 in range(8):
                    p.emit('vector', lambda v, g4=g4: v.tensor_tensor(
                        Egrp[:, g4, :].rearrange("p (a b) -> p a b", a=4, b=128),
                        Egrp[:, g4, :].rearrange("p (a b) -> p a b", a=4, b=128),
                        GTm[:, :].unsqueeze(1).broadcast_to([128, 4, 128]), ALU.mult),
                        wait=([('CP', ec_cp)] if g4 == 0 else ()))
                mt_cp = p.n('CP')
                w = [('SC', conv_sc), ('CP', p.n('CP'))]
                if xst_war:
                    w.append(xst_war)
                for j in range(16):
                    p.emit('tensor', lambda t, j=j, c4=c4: t.transpose(
                        p4[:, j * 128:(j + 1) * 128],
                        xbcs[:, j, c4 * 128:(c4 + 1) * 128], idn[:, :]),
                        wait=(w if j == 0 else ()),
                        inc=('MM' if j == 15 else None))
                mm_tr = p.n('MM')
                for j in range(16):
                    p.emit('vector', lambda v, j=j: v.tensor_copy(
                        xst[:, j, :], p4[:, j * 128:(j + 1) * 128]),
                        wait=([('MM', mm_tr)] if j == 0 else ()))
                xst_cp = p.n('CP')
                p.emit('tensor', lambda t, c4=c4: t.transpose(
                    p4[:, 0:64], xbcs[0:64, 16, c4 * 128:(c4 + 1) * 128], idn[0:64, 0:64]),
                    wait=[('CP', xst_cp)], inc='MM')
                p.emit('vector', lambda v: v.tensor_copy(btm[:, :], p4[:, 0:64]),
                       wait=[('MM', p.n('MM'))])
                p.emit('vector', lambda v: v.tensor_tensor(
                    bw[:, :, :],
                    btm[:, :].unsqueeze(1).broadcast_to([128, 32, 64]),
                    Ecols[:, :].unsqueeze(2).broadcast_to([128, 32, 64]), ALU.mult))
                bw_cp = p.n('CP')
                w = [('SC', p.n('SC'))]
                if ('p2l',) in pm:
                    w.append(pm[('p2l',)])
                p.emit('tensor', lambda t: t.matmul(
                    p2[0:64, 384:416], ones1[0:1, 0:64], lrow[0:1, :],
                    start=True, stop=True), wait=w, inc='MM')
                p.emit('vector', lambda v: v.tensor_copy(lc64[:, :], p2[0:64, 384:416]),
                       wait=[('MM', p.n('MM'))])
                pm[('p2l',)] = ('CP', p.n('CP'))
                lc_cp = p.n('CP')
                for nn2 in range(4):
                    par = nn2 % 2
                    bank = par * 512
                    w = [('LW', lwx)]
                    if ('p1', par) in pm:
                        w.append(pm[('p1', par)])
                    for k in range(8):
                        p.emit('tensor', lambda t, c4=c4, nn2=nn2, k=k, bank=bank,
                               slot=slot: t.matmul(
                            p1[:, bank:bank + 512],
                            xT[:, slot, k, c4 * 128:(c4 + 1) * 128],
                            win[:, k, nn2 * 512:(nn2 + 1) * 512],
                            start=(k == 0), stop=(k == 7)),
                            wait=(w if k == 0 else ()),
                            inc=('MM' if k == 7 else None))
                    mmv = p.n('MM')
                    p.emit('vector', lambda v, nn2=nn2, bank=bank: v.tensor_copy(
                        zt[:, nn2 * 512:(nn2 + 1) * 512], p1[:, bank:bank + 512]),
                        wait=[('MM', mmv)])
                    pm[('p1', par)] = ('CP', p.n('CP'))
                for hh in range(2):
                    h0 = hh * 16
                    w_in = [('CP', mt_cp), ('CP', xst_cp)]
                    if ('p1s',) in pm:
                        w_in.append(pm[('p1s',)])
                    w_in.append(pm[('p1', 0)])
                    w_in.append(pm[('p1', 1)])
                    w_int = [('SC', conv_sc), ('GP', p.n('GP')), ('CP', p.n('CP'))]
                    if ('p2s',) in pm:
                        w_int.append(pm[('p2s',)])
                    w_st = [('CP', bw_cp)]
                    if ('p3s',) in pm:
                        w_st.append(pm[('p3s',)])
                    for hl in range(16):
                        h = h0 + hl
                        p.emit('tensor', lambda t, h=h, hl=hl, c4=c4: t.matmul(
                            p1[:, hl * 64:(hl + 1) * 64],
                            Egrp[:, h // 4, (h % 4) * 128:(h % 4 + 1) * 128],
                            xst[:, h // 2, (h % 2) * 64:(h % 2) * 64 + 64],
                            start=True, stop=True),
                            wait=(w_in if hl == 0 else ()),
                            inc=('MM' if hl == 15 else None))
                    for hl in range(16):
                        h = h0 + hl
                        p.emit('tensor', lambda t, h=h, hl=hl, c4=c4: t.matmul(
                            p2[:, hl * 64:(hl + 1) * 64],
                            cf0[:, :],
                            hTb[:, h, :],
                            start=True, stop=True),
                            wait=(w_int if hl == 0 else ()),
                            inc=('MM' if hl == 15 else None))
                    for hl in range(16):
                        h = h0 + hl
                        p.emit('tensor', lambda t, h=h, hl=hl: t.matmul(
                            p3[0:64, hl * 64:(hl + 1) * 64],
                            bw[:, h, :],
                            xst[:, h // 2, (h % 2) * 64:(h % 2) * 64 + 64],
                            start=True, stop=True),
                            wait=(w_st if hl == 0 else ()),
                            inc=('MM' if hl == 15 else None))
                    mm_sc = p.n('MM')
                    yw = [('MM', mm_sc)]
                    if ('ysb_st',) in pm:
                        yw.append(pm[('ysb_st',)])
                    p.emit('vector', lambda v, hh=hh, c4=c4, h0=h0: v.tensor_tensor(
                        yscr[:, :].rearrange("p (a b) -> p a b", a=16, b=64),
                        p2[:, 0:1024].rearrange("p (a b) -> p a b", a=16, b=64),
                        lamT[:, c4, h0:h0 + 16].unsqueeze(2).broadcast_to([128, 16, 64]),
                        ALU.mult), wait=yw)
                    p.emit('vector', lambda v, hh=hh: v.tensor_tensor(
                        Ysb[:, hh * 1024:(hh + 1) * 1024],
                        yscr[:, :], p1[:, 0:1024], ALU.add))
                    pm[('p1s',)] = ('CP', p.n('CP'))
                    pm[('p2s',)] = ('CP', p.n('CP'))
                    p.emit('vector', lambda v, h0=h0: v.tensor_tensor(
                        hT[:, h0:h0 + 16, :], hT[:, h0:h0 + 16, :],
                        lc64[:, h0:h0 + 16].unsqueeze(2).broadcast_to([64, 16, 64]),
                        ALU.mult), wait=[('CP', lc_cp)])
                    p.emit('vector', lambda v, h0=h0: v.tensor_tensor(
                        hT[:, h0:h0 + 16, :], hT[:, h0:h0 + 16, :],
                        p3[0:64, 0:1024].rearrange("p (a b) -> p a b", a=16, b=64),
                        ALU.add), inc='CP')
                    pm[('p3s',)] = ('CP', p.n('CP'))
                    p.emit('gpsimd', lambda g_, h0=h0: g_.tensor_copy(
                        hTb[:, h0:h0 + 16, :], hT[:, h0:h0 + 16, :]),
                        wait=[('CP', p.n('CP'))], inc='GP')
                y_cp = p.n('CP')
                p.emit('scalar', lambda s: s.activation(
                    zt[:, :], zt[:, :], AF.Silu), wait=[('CP', y_cp)], inc='SC')
                zs_sc = p.n('SC')
                p.emit('gpsimd', lambda g_: g_.tensor_tensor(
                    ynm[:, :].rearrange("p (a b) -> p a b", a=32, b=64),
                    xst[:, :, :].rearrange("p a b -> p (a b)").rearrange(
                        "p (a b) -> p a b", a=32, b=64),
                    dskc[:, :].unsqueeze(2).broadcast_to([128, 32, 64]), ALU.mult),
                    wait=[('CP', y_cp), ('GP', gp0)], inc='GP')
                p.emit('vector', lambda v: v.tensor_tensor(
                    Ysb[:, :], Ysb[:, :], ynm[:, :], ALU.add),
                    wait=[('GP', p.n('GP'))])
                p.emit('vector', lambda v: v.tensor_tensor(
                    Ysb[:, :], Ysb[:, :], zt[:, :], ALU.mult),
                    wait=[('SC', zs_sc)], inc='CP')
                p.emit('scalar', lambda s: s.activation(
                    ynm[:, :], Ysb[:, :], AF.Square, accum_out=stat[:, 0:1]),
                    wait=[('CP', p.n('CP'))])
                p.emit('scalar', lambda s: s.activation(
                    stat[:, 1:2], stat[:, 0:1], AF.Ln, bias=stat[:, 4:5],
                    scale=1.0 / 2048.0))
                p.emit('scalar', lambda s: s.activation(
                    stat[:, 1:2], stat[:, 1:2], AF.Exp, scale=-0.5), inc='SC')
                p.emit('vector', lambda v: v.tensor_scalar(
                    ynm[:, :], Ysb[:, :], stat[:, 1:2], None, ALU.mult),
                    wait=[('SC', p.n('SC'))], inc='CP')
                p.emit('gpsimd', lambda g_: g_.tensor_tensor(
                    ynm[:, :], ynm[:, :], nwb[:, :], ALU.mult),
                    wait=[('CP', p.n('CP'))], inc='GP')
                yn_gp = p.n('GP')
                w = [('GP', yn_gp)]
                for j in range(16):
                    p.emit('tensor', lambda t, j=j: t.transpose(
                        p4[:, j * 128:(j + 1) * 128],
                        ynm[:, j * 128:(j + 1) * 128], idn[:, :]),
                        wait=(w if j == 0 else ()),
                        inc=('MM' if j == 15 else None))
                mm_tr2 = p.n('MM')
                for j in range(16):
                    p.emit('vector', lambda v, j=j, c4=c4: v.tensor_copy(
                        ytf[:, j, c4 * 128:(c4 + 1) * 128], p4[:, j * 128:(j + 1) * 128]),
                        wait=([('MM', mm_tr2)] if j == 0 else ()))
                xst_war = ('CP', p.n('CP'))
            ytf_cp = p.n('CP')
            for kh in range(2):
                wl = [wouth_war] if wouth_war else []
                p.dma('gpsimd', wouth[:, :, :], woutv[kh], wait=wl, inc='LW')
                lww = p.n('LW')
                for mt in range(8):
                    par = mt % 2
                    bank = par * 512
                    w = [('CP', ytf_cp), ('LW', lww)]
                    if ('p1', par) in pm:
                        w.append(pm[('p1', par)])
                    for k in range(8):
                        p.emit('tensor', lambda t, mt=mt, k=k, bank=bank, kh=kh: t.matmul(
                            p1[:, bank:bank + 256],
                            wouth[:, k, mt * 128:(mt + 1) * 128],
                            ytf[:, kh * 8 + k, :],
                            start=(k == 0), stop=(k == 7)),
                            wait=(w if k == 0 else ()),
                            inc=('MM' if k == 7 else None))
                    mmv = p.n('MM')
                    if kh == 0:
                        p.emit('vector', lambda v, mt=mt, bank=bank: v.tensor_copy(
                            ppre[:, mt, :], p1[:, bank:bank + 256]),
                            wait=[('MM', mmv)])
                    else:
                        p.emit('vector', lambda v, mt=mt, bank=bank: v.tensor_tensor(
                            ppre[:, mt, :], p1[:, bank:bank + 256],
                            ppre[:, mt, :], ALU.add),
                            wait=[('MM', mmv)])
                    pm[('p1', par)] = ('CP', p.n('CP'))
                wouth_war = ('MM', p.n('MM'))
            pp_cp = p.n('CP')
            for i4 in range(2):
                for nn2 in range(2):
                    w = [('CP', pp_cp)]
                    if ('p2w', nn2) in pm:
                        w.append(pm[('p2w', nn2)])
                    for k in range(8):
                        p.emit('tensor', lambda t, i4=i4, nn2=nn2, k=k: t.matmul(
                            p2[:, nn2 * 512:(nn2 + 1) * 512],
                            ppre[:, k, i4 * 128:(i4 + 1) * 128],
                            woh[:, k, nn2 * 512:(nn2 + 1) * 512],
                            start=(k == 0), stop=(k == 7)),
                            wait=(w if k == 0 else ()),
                            inc=('MM' if k == 7 else None))
                    mmv = p.n('MM')
                    pw = [('MM', mmv)]
                    if i4 == 1 and nn2 == 0:
                        pw.append(('ST', p.n('ST')))
                    p.emit('vector', lambda v, i4=i4, nn2=nn2: v.tensor_copy(
                        Ysb[:, nn2 * 512:(nn2 + 1) * 512],
                        p2[:, nn2 * 512:(nn2 + 1) * 512]),
                        wait=pw)
                    pm[('p2w', nn2)] = ('CP', p.n('CP'))
                p.dma('sync', prv[g][:, i4, :], Ysb[:, 0:1024],
                      wait=[('CP', p.n('CP'))], inc='ST')
                pm[('ysb_st',)] = ('ST', p.n('ST'))
            xT_war[slot] = ('MM', p.n('MM'))
        p.emit('sync', lambda q: q.nop(), wait=[('ST', p.n('ST'))])
        p.finalize(blk)
    return nc


_NCCACHE = {}


def _get(name, builder):
    if name not in _NCCACHE:
        _NCCACHE[name] = builder()
    return _NCCACHE[name]


def _f32(a):
    return np.ascontiguousarray(a, dtype=np.float32)


def kernel(**inp):
    idn = np.eye(128, dtype=np.float32)
    strU = np.triu(np.ones((128, 128), np.float32), 1)
    x0 = _f32(inp["L0_attn_x"]).reshape(-1, D)
    Wo = inp["L0_attn_Wo"]

    def colsplit(b, nm):
        return _f32(np.asarray(b).reshape(nm, 128).T)

    def a_inmap(b, rev):
        pre = "b" if rev else "f"
        xx = x0.reshape(4, 4096, D)[b]
        if rev:
            xx = xx[::-1]
        convw = np.asarray(inp[f"L0_attn_{pre}convw"])
        cw = _f32(np.transpose(convw[:, 0, :].reshape(4, 17, 128), (2, 1, 0)).reshape(128, 68))
        woh = Wo[1024:] if rev else Wo[:1024]
        return dict(
            xTd=_f32(xx.T),
            wind=_f32(inp[f"L0_attn_{pre}Win"]),
            convwd=cw,
            convbd=colsplit(inp[f"L0_attn_{pre}convb"], 17),
            dtbd=_f32(np.broadcast_to(inp[f"L0_attn_{pre}dtb"], (128, 32))),
            alogd=_f32(np.asarray(inp[f"L0_attn_{pre}Alog"]).reshape(1, 32)),
            dskcd=_f32(np.broadcast_to(inp[f"L0_attn_{pre}D"], (128, 32))),
            nwd=_f32(np.broadcast_to(inp[f"L0_attn_{pre}normw"], (128, 2048))),
            woutd=_f32(inp[f"L0_attn_{pre}Wout"]),
            wohd=_f32(woh),
            idnd=idn, incUd=np.triu(np.ones((128, 128), np.float32), 0),
            triLd=np.tril(np.full((128, 128), -1e9, np.float32), -1),
        )

    ncA = _get("A", build_phaseA)
    resA = run_bass_kernel_spmd(ncA, [a_inmap(c // 2, bool(c % 2)) for c in range(8)],
                                core_ids=list(range(8)))
    Pf = np.concatenate([resA.results[2 * b]["prowd"] for b in range(4)], 0)
    Pb = np.concatenate([resA.results[2 * b + 1]["prowd"][::-1] for b in range(4)], 0)

    ncAB1 = _get("AB1", build_phaseAB1)
    in_maps = []
    for c in range(8):
        sl = slice(c * NTOK, (c + 1) * NTOK)
        in_maps.append(dict(
            x0r=_f32(x0[sl]), pfr=_f32(Pf[sl]), pbr=_f32(Pb[sl]),
            borow=_f32(np.asarray(inp["L0_attn_bo"]).reshape(1, D)),
            gTd=_f32(np.asarray(inp["L0_ffn_gate_w"]).T),
            idnd=idn,
        ))
    res1 = run_bass_kernel_spmd(ncAB1, in_maps, core_ids=list(range(8)))
    x1 = np.concatenate([r["x1r"] for r in res1.results], 0)
    logs = np.concatenate([r["logr"] for r in res1.results], 0)
    cnts = np.stack([r["cnt"][0] for r in res1.results], 0)

    ncB2 = _get("B2", build_phaseB2)
    in_maps = []
    for c in range(8):
        sl = slice(c * NTOK, (c + 1) * NTOK)
        mask = np.zeros((8, 1), np.float32)
        mask[:c] = 1.0
        in_maps.append(dict(
            x1r=_f32(x1[sl]), logrd=_f32(logs[sl]), cntalld=_f32(cnts),
            maskcold=mask,
            wegT=_f32(np.transpose(inp["L0_ffn_we_gate"], (0, 2, 1))),
            weuT=_f32(np.transpose(inp["L0_ffn_we_up"], (0, 2, 1))),
            wedT=_f32(np.transpose(inp["L0_ffn_we_down"], (0, 2, 1))),
            wsgT=_f32(np.asarray(inp["L0_ffn_ws_gate"]).T),
            wsuT=_f32(np.asarray(inp["L0_ffn_ws_up"]).T),
            wsdT=_f32(np.asarray(inp["L0_ffn_ws_down"]).T),
            idnd=idn, strUd=strU,
        ))
    res2 = run_bass_kernel_spmd(ncB2, in_maps, core_ids=list(range(8)))
    x2 = np.concatenate([r["x2r"] for r in res2.results], 0)

    qkv_w = np.asarray(inp["L1_attn_qkv_w"])
    qkv_b = np.asarray(inp["L1_attn_qkv_b"])
    ncC1 = _get("C1", build_phaseC1)
    in_maps = []
    for c in range(8):
        sl = slice(c * NTOK, (c + 1) * NTOK)
        in_maps.append(dict(
            x2r=_f32(x2[sl]),
            wqT=_f32(qkv_w[0:1024].T), wkT=_f32(qkv_w[1024:2048].T),
            wvT=_f32(qkv_w[2048:3072].T),
            qbc=colsplit(qkv_b[0:1024], 8), kbc=colsplit(qkv_b[1024:2048], 8),
            vbb=_f32(np.broadcast_to(qkv_b[2048:3072], (128, D))),
            idnd=idn,
        ))
    resC1 = run_bass_kernel_spmd(ncC1, in_maps, core_ids=list(range(8)))
    kf = [r["kfo"] for r in resC1.results]
    vt = [r["vto"] for r in resC1.results]
    qf = [r["qfo"] for r in resC1.results]

    mkC = np.zeros((128, 4, 512), np.float32)
    for qs in range(4):
        for p_ in range(128):
            mkC[p_, qs, qs * 128 + p_ + 1:] = -1e9
    mkT = np.zeros((128, 4, 512), np.float32)
    for kc in range(4):
        for p_ in range(128):
            mkT[p_, kc, :kc * 128 + p_] = -1e9
    kfull = [np.concatenate([kf[2 * b], kf[2 * b + 1]], axis=1) for b in range(4)]
    ncC2 = _get("C2", build_phaseC2)
    in_maps = []
    for c in range(8):
        b, hg = c // 2, c % 2
        vfull_b = np.concatenate([vt[2 * b], vt[2 * b + 1]], axis=0)
        in_maps.append(dict(
            khTd=_f32(kfull[b][hg * 512:(hg + 1) * 512]),
            vsld=_f32(vfull_b[:, hg * 512:(hg + 1) * 512]),
            qhTd=_f32(qf[2 * b + 1][hg * 512:(hg + 1) * 512]),
            mkCd=_f32(mkC.reshape(128, 2048)), mkTd=_f32(mkT.reshape(128, 2048)),
        ))
    resC2 = run_bass_kernel_spmd(ncC2, in_maps, core_ids=list(range(8)))
    oo = [r["ood"] for r in resC2.results]
    o_all = np.concatenate([np.concatenate([oo[2 * b], oo[2 * b + 1]], axis=1)
                            for b in range(4)], axis=0)

    ncC3 = _get("C3", build_phaseC3)
    in_maps = [dict(orows=_f32(o_all[c * 256:(c + 1) * 256]),
                    pwT=_f32(qkv_w[0:0] if False else np.asarray(inp["L1_attn_proj_w"]).T),
                    idnd=idn) for c in range(8)]
    resC3 = run_bass_kernel_spmd(ncC3, in_maps, core_ids=list(range(8)))
    apro = np.concatenate([r["apro"] for r in resC3.results], 0).reshape(4, 512, D)

    attn = np.zeros((16384, D), np.float32)
    for b in range(4):
        attn[b * 4096 + 3584:(b + 1) * 4096] = apro[b]
    ncD = _get("D", build_phaseD)
    in_maps = []
    for c in range(8):
        sl = slice(c * NTOK, (c + 1) * NTOK)
        in_maps.append(dict(
            x2r=_f32(x2[sl]), atr=_f32(attn[sl]),
            pbrow=_f32(np.asarray(inp["L1_attn_proj_b"]).reshape(1, D)),
            w1T=_f32(inp["L1_ffn_w1"]), b1cd=colsplit(inp["L1_ffn_b1"], 32),
            w2T=_f32(inp["L1_ffn_w2"]), b2cd=colsplit(inp["L1_ffn_b2"], 8),
            idnd=idn,
        ))
    resD = run_bass_kernel_spmd(ncD, in_maps, core_ids=list(range(8)))
    out = np.concatenate([r["outr"] for r in resD.results], 0)
    return out.reshape(4, 4096, D).astype(np.float32)
